# version 01 (final)
# baseline (speedup 1.0000x reference)
import numpy as np
import ml_dtypes
import concourse.bass as bass
import concourse.mybir as mybir
from concourse.bass_utils import run_bass_kernel_spmd

F32 = mybir.dt.float32
BF16 = mybir.dt.bfloat16
I32 = mybir.dt.int32
AF = mybir.ActivationFunctionType
ALU = mybir.AluOpType
AXX = mybir.AxisListType.X

ENGS = ("pe", "act", "dve", "pool", "sp")
DMAQ = ("dpool", "dsp")
EPOCH = {"dpool": "epool", "dsp": "esp"}
EPOCH_Q = {"dpool": "pool", "dsp": "sp"}
_BLOCK_ATTR = {"pe": "tensor", "act": "scalar", "dve": "vector", "pool": "gpsimd", "sp": "sync"}

SBUF_BASE = 16576
SBUF_BYTES = 229344


def bf16(x):
    return np.asarray(x).astype(ml_dtypes.bfloat16)



class View:
    __slots__ = ("ap", "key", "rect")

    def __init__(self, ap, key, rect):
        self.ap = ap
        self.key = key
        self.rect = rect


class Buf:

    def __init__(self, nc, name, shape, dtype, offset):
        self.h = nc.alloc_sbuf_tensor_at(name, list(shape), dtype, offset=offset)
        self.name = name
        self.shape = tuple(shape)
        self.dtype = dtype

    def __getitem__(self, idx):
        if not isinstance(idx, tuple):
            idx = (idx,)
        ap = self.h[idx]
        free_shape = self.shape[1:]
        lo, hi = 0, 1
        tot = 1
        for d in free_shape:
            tot *= d
        lo, hi = self._free_range(idx, free_shape)
        return View(ap, self.name, (lo, hi))

    def _free_range(self, idx, free_shape):
        strides = []
        s = 1
        for d in reversed(free_shape):
            strides.append(s)
            s *= d
        strides.reverse()
        lo, hi = 0, 0
        full = True
        for k, d in enumerate(free_shape):
            i = idx[k + 1] if k + 1 < len(idx) else slice(None)
            if isinstance(i, slice):
                a, b, st = i.indices(d)
                assert st == 1
            else:
                a, b = int(i), int(i) + 1
            lo += a * strides[k]
            hi += (b - 1) * strides[k]
        return lo, hi + 1

    @property
    def v(self):
        ap = self.h[tuple(slice(None) for _ in self.shape)]
        return View(ap, self.name, None)


class PsumBuf:

    def __init__(self, nc):
        self.f32 = nc.alloc_psum_tensor("psum_all", [128, 4096], F32)
        self.b16 = nc.virtual_psum_tensor("psum_bf16", [128, 8192], BF16)
        nc.back_virtual_tensor_with(self.b16, self.f32)
        self.name = "PSUM"

    def bank(self, b, p0=0, p1=128, c0=0, c1=512):
        ap = self.f32[p0:p1, b * 512 + c0: b * 512 + c1]
        return View(ap, self.name, (b * 2048, (b + 1) * 2048))

    def banks(self, b0, nb, p0=0, p1=128, c0=0, c1=None):
        if c1 is None:
            c1 = nb * 512
        ap = self.f32[p0:p1, b0 * 512 + c0: b0 * 512 + c1]
        return View(ap, self.name, (b0 * 2048, (b0 + nb) * 2048))

    def bank16(self, b, p0=0, p1=128, c0=0, c1=1024):
        ap = self.b16[p0:p1, b * 1024 + c0: b * 1024 + c1]
        return View(ap, self.name, (b * 2048, (b + 1) * 2048))


class DBuf:

    def __init__(self, nc, name, shape, dtype, kind="Internal"):
        self.t = nc.dram_tensor(name, list(shape), dtype, kind=kind)
        self.name = name
        self.shape = tuple(shape)
        self.dtype = dtype

    def __getitem__(self, idx):
        if not isinstance(idx, tuple):
            idx = (idx,)
        ap = self.t.ap()[idx]
        strides = []
        s = 1
        for d in reversed(self.shape):
            strides.append(s)
            s *= d
        strides.reverse()
        lo = hi = 0
        for k, d in enumerate(self.shape):
            i = idx[k] if k < len(idx) else slice(None)
            if isinstance(i, slice):
                a, b, st = i.indices(d)
                assert st == 1
            else:
                a, b = int(i), int(i) + 1
            lo += a * strides[k]
            hi += (b - 1) * strides[k]
        return View(ap, self.name, (lo, hi + 1))

    @property
    def v(self):
        return View(self.t.ap(), self.name, None)

    def rearranged(self, pattern, rect=None, **kw):
        return View(self.t.ap().rearrange(pattern, **kw), self.name, rect)


class Mem:

    def __init__(self, nc):
        self.nc = nc
        self.off = SBUF_BASE
        self.n = 0

    def alloc(self, name, shape, dtype, align=64):
        self.off = (self.off + align - 1) // align * align
        nbytes = int(np.prod(shape[1:])) * mybir.dt.size(dtype)
        assert self.off + nbytes <= SBUF_BYTES, (
            f"SBUF overflow: {name} {shape} at {self.off} + {nbytes} > {SBUF_BYTES}")
        self.n += 1
        b = Buf(self.nc, f"{name}_{self.n}", shape, dtype, self.off)
        self.off += nbytes
        return b

    def mark(self):
        return self.off

    def reset(self, m):
        self.off = m



class Op:
    __slots__ = ("eng", "emit", "idx", "eidx", "deps", "marked", "inc_value",
                 "dma_q", "dma_seq", "clock", "tag", "fence")

    def __init__(self, eng, emit):
        self.eng = eng
        self.emit = emit
        self.deps = []
        self.marked = False
        self.dma_q = None
        self.dma_seq = 0
        self.clock = None
        self.tag = ""
        self.fence = None


def _overlap(r1, r2):
    if r1 is None or r2 is None:
        return True
    return r1[0] < r2[1] and r2[0] < r1[1]


class Prog:
    def __init__(self, nc):
        self.nc = nc
        self.ops = []
        self.eng_ops = {e: [] for e in ENGS}
        self.eng_clock = {e: {} for e in ENGS}
        self.reg = {}
        self.dma_count = {q: 0 for q in DMAQ}
        self.fence_count = {q: 0 for q in DMAQ}
        self.fence_seq = {q: 0 for q in DMAQ}
        self.fence_ops = {q: None for q in DMAQ}
        self.out_dmas = []
        self.same_engine_sync = False
        self.cur_tag = ""


    def _dep_token(self, op):
        if op.fence is not None:
            return (EPOCH[op.fence[0]], op.fence[2])
        if op.dma_q:
            return (op.dma_q, op.dma_seq)
        return (op.eng, op.idx)

    def _get_fence(self, q, upto):
        if self.fence_count[q] >= upto and self.fence_ops[q] is not None:
            return self.fence_ops[q]
        n = self.dma_count[q]
        self.fence_seq[q] += 1
        self.fence_count[q] = n
        def em(eng):
            return eng.nop()
        op = Op(EPOCH_Q[q], em)
        op.idx = len(self.ops)
        op.tag = "fence"
        op.fence = (q, 16 * n, self.fence_seq[q])
        op.marked = True
        self.ops.append(op)
        op.eidx = len(self.eng_ops[EPOCH_Q[q]])
        self.eng_ops[EPOCH_Q[q]].append(op)
        cl = dict(self.eng_clock[EPOCH_Q[q]])
        cl[EPOCH[q]] = self.fence_seq[q]
        cl[q] = n
        op.clock = cl
        self.fence_ops[q] = op
        return op

    def _add_dep(self, op, prod):
        if prod is op:
            return
        if prod.dma_q is not None:
            prod = self._get_fence(prod.dma_q, prod.dma_seq)
            if prod is op:
                return
        src, val = self._dep_token(prod)
        if prod.dma_q is None and prod.eng == op.eng and op.eng == "pe":
            self._merge_clock(op, prod)
            return
        cl = self.eng_clock[op.eng]
        if cl.get(src, -1) >= val:
            return
        op.deps.append(prod)
        prod.marked = True
        self._merge_clock(op, prod)
        cl[src] = val

    def _merge_clock(self, op, prod):
        cl = self.eng_clock[op.eng]
        if prod.clock:
            for s, v in prod.clock.items():
                if cl.get(s, -1) < v:
                    cl[s] = v

    def _op(self, eng, emit, reads=(), writes=(), dma_q=None, tag=None):
        op = Op(eng, emit)
        op.idx = len(self.ops) + 4096
        op.tag = tag or self.cur_tag
        op.dma_q = dma_q

        for v in reads:
            if v is None:
                continue
            lst = self.reg.get(v.key)
            if lst:
                for rect, o2, w2 in lst:
                    if w2 and _overlap(v.rect, rect):
                        self._add_dep(op, o2)
        for v in writes:
            if v is None:
                continue
            lst = self.reg.get(v.key)
            if lst:
                for rect, o2, w2 in lst:
                    if _overlap(v.rect, rect):
                        self._add_dep(op, o2)

        if dma_q:
            self.dma_count[dma_q] += 1
            op.dma_seq = self.dma_count[dma_q]
        op.idx = len(self.ops)
        self.ops.append(op)
        op.eidx = len(self.eng_ops[eng])
        self.eng_ops[eng].append(op)
        cl = dict(self.eng_clock[eng])
        tok = self._dep_token(op)
        cl[tok[0]] = tok[1]
        op.clock = cl

        for v in reads:
            if v is not None:
                self.reg.setdefault(v.key, []).append((v.rect, op, False))
        for v in writes:
            if v is not None:
                self.reg.setdefault(v.key, []).append((v.rect, op, True))
        for key in {v.key for v in list(reads) + list(writes) if v is not None}:
            lst = self.reg[key]
            if len(lst) > 192:
                dropped = lst[:-96]
                best = {}
                for rect, o2, w2 in dropped:
                    src = o2.dma_q or o2.eng
                    if src not in best or best[src].idx < o2.idx:
                        best[src] = o2
                self.reg[key] = [(None, o2, True) for o2 in best.values()] + lst[-96:]
        return op

    def barrier(self):
        for q in DMAQ:
            if self.dma_count[q]:
                self._get_fence(q, self.dma_count[q])
        lasts = {e: (self.eng_ops[e][-1] if self.eng_ops[e] else None) for e in ENGS}
        for e in ENGS:
            def em(eng):
                return eng.nop()
            op = self._op(e, em, (), (), tag="barrier")
            for e2, o2 in lasts.items():
                if o2 is not None and e2 != e:
                    self._add_dep(op, o2)
        for q in DMAQ:
            last = None
            for o in reversed(self.ops):
                if o.dma_q == q:
                    last = o
                    break
            if last is not None:
                for e in ENGS:
                    op = self.eng_ops[e][-1]
                    if op.tag == "barrier":
                        self._add_dep(op, last)


    def dma(self, dst, src, queue="sp", output=False):
        def em(eng):
            return eng.dma_start(dst.ap, src.ap)
        q = "dpool" if queue == "pool" else "dsp"
        op = self._op(queue, em, (src,), (dst,), dma_q=q)
        if output:
            self.out_dmas.append(op)
        return op

    def mm(self, out, lhsT, rhs, start=True, stop=True):
        def em(eng):
            return eng.matmul(out.ap, lhsT.ap, rhs.ap, start=start, stop=stop)
        return self._op("pe", em, (lhsT, rhs), (out,))

    def transpose(self, out, in_, ident):
        def em(eng):
            return eng.transpose(out.ap, in_.ap, ident.ap)
        return self._op("pe", em, (in_, ident), (out,))

    def act(self, out, in_, func, bias=0.0, scale=1.0, accum=None):
        bview = bias if isinstance(bias, View) else None
        sview = scale if isinstance(scale, View) else None
        def em(eng):
            return eng.activation(
                out.ap, in_.ap, func,
                bias=bias.ap if bview is not None else bias,
                scale=scale.ap if sview is not None else scale,
                accum_out=accum.ap if accum is not None else None)
        reads = [in_] + ([bview] if bview else []) + ([sview] if sview else [])
        writes = [out] + ([accum] if accum is not None else [])
        return self._op("act", em, reads, writes)

    def _veng(self, engine):
        return engine

    def tt(self, out, a, b, op, engine="dve"):
        def em(eng):
            return eng.tensor_tensor(out.ap, a.ap, b.ap, op)
        return self._op(engine, em, (a, b), (out,))

    def ts(self, out, a, s1, op0, s2=None, op1=ALU.bypass, engine="dve"):
        s1v = s1 if isinstance(s1, View) else None
        s2v = s2 if isinstance(s2, View) else None
        def em(eng):
            return eng.tensor_scalar(
                out.ap, a.ap,
                s1.ap if s1v is not None else s1,
                (s2.ap if s2v is not None else s2),
                op0, op1)
        reads = [a] + ([s1v] if s1v else []) + ([s2v] if s2v else [])
        return self._op(engine, em, reads, (out,))

    def stt(self, out, in0, scalar, in1, op0, op1, engine="dve"):
        sv = scalar if isinstance(scalar, View) else None
        def em(eng):
            return eng.scalar_tensor_tensor(
                out.ap, in0.ap, scalar.ap if sv is not None else scalar,
                in1.ap, op0, op1)
        reads = [in0, in1] + ([sv] if sv else [])
        return self._op(engine, em, reads, (out,))

    def copy(self, out, in_, engine="dve"):
        def em(eng):
            return eng.tensor_copy(out.ap, in_.ap)
        return self._op(engine, em, (in_,), (out,))

    def reduce(self, out, in_, op, negate=False, engine="dve"):
        def em(eng):
            return eng.tensor_reduce(out.ap, in_.ap, AXX, op, negate=negate)
        return self._op(engine, em, (in_,), (out,))

    def recip(self, out, in_):
        def em(eng):
            return eng.reciprocal(out.ap, in_.ap)
        return self._op("dve", em, (in_,), (out,))

    def memset(self, dst, val, engine="dve"):
        def em(eng):
            return eng.memset(dst.ap, val)
        return self._op(engine, em, (), (dst,))

    def select(self, out, mask, on_true, on_false):
        def em(eng):
            return eng.select(out.ap, mask.ap, on_true.ap, on_false.ap)
        return self._op("dve", em, (mask, on_true, on_false), (out,))

    def fence_outputs(self):
        for o in self.out_dmas:
            self._get_fence(o.dma_q, o.dma_seq)
        def em(eng):
            return eng.nop()
        op = self._op("sp", em, (), (), tag="fenceout")
        for o in self.out_dmas:
            self._add_dep(op, o)
        if not op.deps and self.out_dmas:
            o = self.out_dmas[-1]
            op.deps.append(self._get_fence(o.dma_q, o.dma_seq))


    def simulate(self, verbose=True):
        inc_ctr = {e: 0 for e in ENGS}
        for ops in self.eng_ops.values():
            for op in ops:
                if op.dma_q is None and op.fence is None and op.marked:
                    inc_ctr[op.eng] += 1
                    op.inc_value = inc_ctr[op.eng]
        pos = {e: 0 for e in ENGS}
        sem = {}
        def get(k):
            return sem.get(k, 0)
        def waits_ok(op):
            if op.fence is not None:
                return get(op.fence[0]) >= op.fence[1]
            for prod in op.deps:
                if prod.fence is not None:
                    if get(EPOCH[prod.fence[0]]) < prod.fence[2]:
                        return False
                elif prod.dma_q:
                    if get(prod.dma_q) < 16 * prod.dma_seq:
                        return False
                else:
                    if get(prod.eng) < prod.inc_value:
                        return False
            return True
        progress = True
        while progress:
            progress = False
            for e in ENGS:
                while pos[e] < len(self.eng_ops[e]):
                    op = self.eng_ops[e][pos[e]]
                    if not waits_ok(op):
                        break
                    if op.fence is not None:
                        sem[EPOCH[op.fence[0]]] = get(EPOCH[op.fence[0]]) + 1
                    elif op.dma_q:
                        sem[op.dma_q] = get(op.dma_q) + 16
                    elif op.marked:
                        sem[op.eng] = get(op.eng) + 1
                    pos[e] += 1
                    progress = True
        stuck = {e: len(self.eng_ops[e]) - pos[e] for e in ENGS if pos[e] < len(self.eng_ops[e])}
        if stuck and verbose:
            print("DEADLOCK: remaining per engine:", stuck)
            for e in stuck:
                op = self.eng_ops[e][pos[e]]
                info = []
                if op.fence is not None:
                    info.append(f"fence wait {op.fence[0]}>={op.fence[1]} (have {get(op.fence[0])})")
                for prod in op.deps:
                    if prod.fence is not None:
                        info.append(f"epoch {EPOCH[prod.fence[0]]}>={prod.fence[2]} (have {get(EPOCH[prod.fence[0]])})")
                    elif prod.dma_q:
                        info.append(f"{prod.dma_q}>={16*prod.dma_seq} (have {get(prod.dma_q)})")
                    else:
                        info.append(f"S_{prod.eng}>={prod.inc_value} (have {get(prod.eng)})")
                print(f"  {e}[{pos[e]}] idx={op.idx} tag={op.tag} blocked on {info}")
        return not stuck


    def emit(self):
        assert self.simulate(), "schedule deadlock detected"
        nc = self.nc
        inc_ctr = {e: 0 for e in ENGS}
        for ops in self.eng_ops.values():
            for op in ops:
                if op.dma_q is None and op.fence is None and op.marked:
                    inc_ctr[op.eng] += 1
                    op.inc_value = inc_ctr[op.eng]
        from contextlib import ExitStack
        with ExitStack() as st:
            sems = {}
            for e in ENGS:
                sems[e] = st.enter_context(nc.semaphore(f"S_{e}"))
            for q in DMAQ:
                sems[q] = st.enter_context(nc.semaphore(f"S_{q}"))
                sems[EPOCH[q]] = st.enter_context(nc.semaphore(f"S_{EPOCH[q]}"))
            blk = st.enter_context(nc.Block())
            for ename in ENGS:
                eops = self.eng_ops[ename]
                if not eops:
                    continue
                def body(eng, eops=eops):
                    for op in eops:
                        if op.fence is not None:
                            q, val, seq = op.fence
                            eng.wait_ge(sems[q], val)
                            eng.nop().then_inc(sems[EPOCH[q]], 1)
                            continue
                        for prod in op.deps:
                            if prod.fence is not None:
                                eng.wait_ge(sems[EPOCH[prod.fence[0]]], prod.fence[2])
                            elif prod.dma_q:
                                eng.wait_ge(sems[prod.dma_q], 16 * prod.dma_seq)
                            else:
                                eng.wait_ge(sems[prod.eng], prod.inc_value)
                        inst = op.emit(eng)
                        if op.dma_q:
                            inst.then_inc(sems[op.dma_q], 16)
                        elif op.marked:
                            inst.then_inc(sems[op.eng], 1)
                getattr(blk, _BLOCK_ATTR[ename])(body)
        return nc



def make_consts():
    c = {}
    c["ones_col_b"] = np.ones((128, 1), ml_dtypes.bfloat16)
    c["ones_col_f"] = np.ones((128, 1), np.float32)
    c["ones_row_f"] = np.ones((1, 128), np.float32)
    c["ones_row_b"] = np.ones((1, 128), ml_dtypes.bfloat16)
    c["ident_f"] = np.eye(128, dtype=np.float32)
    es = np.zeros((8, 1024), np.float32)
    for e in range(8):
        es[e, e * 128:(e + 1) * 128] = 1.0
    c["esel8_b"] = bf16(es)
    c["ones2_b"] = np.zeros((128, 2), ml_dtypes.bfloat16)
    c["ones2_b"][0:64, 0] = 1
    c["ones2_b"][64:128, 1] = 1
    c["eneg2_f"] = np.zeros((2, 128), np.float32)
    c["eneg2_f"][0, 0:128] = -1.0 / 16.0
    c["eneg2b_f"] = np.zeros((2, 128), np.float32)
    c["eneg2b_f"][1, 0:128] = -1.0 / 16.0
    c["ones64_f"] = np.ones((1, 64), np.float32)
    o65 = np.zeros((65, 64), np.float32)
    o65[0, :] = 1.0
    o65[64, :] = 1.0
    c["ones65_f"] = o65
    rot = np.zeros((128, 128), np.float32)
    for base in (0, 64):
        for j in range(32):
            rot[base + j + 32, base + j] = -1.0
            rot[base + j, base + j + 32] = 1.0
    c["rot_f"] = rot
    inv = 1.0 / (10000.0 ** (np.arange(0, 64, 2, dtype=np.float64) / 64.0))
    t = np.arange(4096, dtype=np.float64)
    emb = np.concatenate([t[:, None] * inv[None, :]] * 2, axis=1)
    cos = np.cos(emb).T.astype(np.float32)
    sin = np.sin(emb).T.astype(np.float32)
    c["l0cos_b"] = bf16(np.concatenate([cos, cos], axis=0))
    c["l0sin_b"] = bf16(np.concatenate([sin, sin], axis=0))
    tri = np.tril(np.ones((128, 128), np.float32)).T
    c["tri_b"] = bf16(tri)
    c["tri2_b"] = bf16(tri.T)
    c["onesr_f"] = np.ones((1, 128), np.float32)
    c["trineg_f"] = np.where(np.tril(np.ones((128, 128), bool)), 0.0, -1e30).astype(np.float32)
    mh0 = np.zeros((128, 1), np.float32); mh0[0:64] = 1.0
    mh1 = np.zeros((128, 1), np.float32); mh1[64:128] = 1.0
    c["mh0_f"] = mh0
    c["mh1_f"] = mh1
    es2 = np.zeros((2, 128), np.float32)
    es2[0, 0:64] = 1.0
    es2[1, 64:128] = 1.0
    c["esel2_f"] = es2
    c["eps_f"] = np.tile(np.array([[1e-5, 1e-6, 1e-24, 1e-30]], np.float32), (128, 1))
    return c


CONST_DTYPES = {"_b": BF16, "_f": F32}


class Globals:

    def __init__(self, P, mem, const_names, hbm_only=()):
        nc = P.nc
        self._init_psum_pending = True
        self.P = P
        self.mem = mem
        self.c = {}
        self.cd = {}
        host = make_consts()
        for n in const_names:
            arr = host[n]
            dt = BF16 if n.endswith("_b") else F32
            d = DBuf(nc, "c_" + n, arr.shape, dt, kind="ExternalInput")
            self.cd[n] = d
            if n not in hbm_only:
                b = mem.alloc("c_" + n, arr.shape, dt)
                P.dma(b.v, d.v, queue="sp")
                self.c[n] = b
        self.host_consts = {("c_" + n): host[n] for n in const_names}
        self.sq = mem.alloc("g_sq", [128, 2, 512], BF16)
        self.tmpa = mem.alloc("g_tmpa", [128, 2, 512], F32)
        self.tmpb = mem.alloc("g_tmpb", [128, 2, 512], F32)
        self.rows = mem.alloc("g_rows", [1, 1024], F32)
        self.psum = PsumBuf(nc)
        P.memset(View(self.psum.f32[:, :], self.psum.name, (0, 16384)), 0.0)


def ln_fm(P, G, xv, outv, T, gains=None, eps_col=0, bank_stats=(2, 3), bank_bc=(4, 5),
          in_f32=False):
    ps = G.psum
    ones = G.c["ones_col_f"] if in_f32 else G.c["ones_col_b"]
    sumv = ps.bank(bank_stats[0], p0=0, p1=1, c0=0, c1=T)
    sqsv = ps.bank(bank_stats[1], p0=0, p1=1, c0=0, c1=T)
    for c in range(8):
        sq = G.sq[:, c % 2, 0:T]
        P.act(sq, xv(c), AF.Square)
        P.mm(sumv, ones[:, :], xv(c), start=(c == 0), stop=(c == 7))
        P.mm(sqsv, G.c["ones_col_b"][:, :], sq, start=(c == 0), stop=(c == 7))
    negmean = G.rows[0:1, 0:T]
    ex2 = G.rows[0:1, 512:512 + T]
    m2 = View(G.tmpb.h[0:1, 0, 0:T], G.tmpb.name, (0, T))
    rstd = View(G.tmpb.h[0:1, 1, 0:T], G.tmpb.name, (512, 512 + T))
    P.ts(negmean, sumv, -1.0 / 1024.0, ALU.mult)
    P.ts(ex2, sqsv, 1.0 / 1024.0, ALU.mult)
    P.tt(m2, negmean, negmean, ALU.mult)
    P.tt(ex2, ex2, m2, ALU.subtract)
    P.act(m2, ex2, AF.Ln, bias=G.c["eps_f"][0:1, eps_col:eps_col + 1])
    P.act(rstd, m2, AF.Exp, scale=-0.5)
    nm_b = ps.bank(bank_bc[0], c0=0, c1=T)
    rs_b = ps.bank(bank_bc[1], c0=0, c1=T)
    P.mm(nm_b, G.c["ones_row_f"][:, :], negmean, start=True, stop=True)
    P.mm(rs_b, G.c["ones_row_f"][:, :], rstd, start=True, stop=True)
    for c in range(8):
        tf = G.tmpa[:, c % 2, 0:T]
        P.tt(tf, xv(c), nm_b, ALU.add)
        if gains is None:
            P.tt(outv(c), tf, rs_b, ALU.mult)
        else:
            tf2 = G.tmpa[:, (c + 1) % 2, 0:T]
            P.tt(tf2, tf, rs_b, ALU.mult)
            gv, bv = gains
            P.ts(outv(c), tf2, gv(c), ALU.mult, bv(c), ALU.add)


def rope_apply(P, G, psum_bank_in, rot_bank, cosv, sinv, outv, T=512):
    tmp = G.tmpa[:, 0, 0:T]
    t1 = G.tmpb[:, 0, 0:T]
    t2 = G.tmpa[:, 1, 0:T]
    P.act(tmp, psum_bank_in, AF.Copy)
    P.mm(rot_bank, G.c["rot_f"].v, tmp, start=True, stop=True)
    P.tt(t1, tmp, cosv, ALU.mult)
    P.tt(t2, rot_bank, sinv, ALU.mult)
    P.tt(outv, t1, t2, ALU.add)



def host_prep_l0_attn(inp):
    return {
        "l0_wqT": np.ascontiguousarray(inp["L0_attn_Wq"].T),
        "l0_wkT": np.ascontiguousarray(inp["L0_attn_Wk"].T),
        "l0_wvT": np.ascontiguousarray(inp["L0_attn_Wv"].T),
        "l0_woT": np.ascontiguousarray(inp["L0_attn_Wo"].T),
    }


def build_l0_attn(P, G, mem, xin, xout, xin_f32):
    nc = P.nc
    ps = G.psum
    wq = DBuf(nc, "l0_wqT", [1024, 1024], F32, kind="ExternalInput")
    wk = DBuf(nc, "l0_wkT", [1024, 1024], F32, kind="ExternalInput")
    wv = DBuf(nc, "l0_wvT", [1024, 1024], F32, kind="ExternalInput")
    wo = DBuf(nc, "l0_woT", [1024, 1024], F32, kind="ExternalInput")
    xnsp = DBuf(nc, "l0_xnspill", [8, 128, 4096], BF16, kind="Internal")

    m0 = mem.mark()
    KT = mem.alloc("l0_KT", [128, 8, 4096], BF16)
    V = mem.alloc("l0_V", [128, 32, 1040], BF16)
    w1 = mem.alloc("l0_w1", [128, 8, 1024], BF16)
    w2 = mem.alloc("l0_w2", [128, 8, 1024], BF16)
    xc = mem.alloc("l0_xc", [128, 8, 512], BF16)
    cosb = mem.alloc("l0_cosb", [128, 1, 512], BF16)
    sinb = mem.alloc("l0_sinb", [128, 1, 512], BF16)
    qt = mem.alloc("l0_qt", [128, 512], BF16)
    pt = mem.alloc("l0_pt", [128, 2048], BF16)
    ot = mem.alloc("l0_ot", [128, 8, 512], BF16)
    otmp = mem.alloc("l0_otmp", [64, 512], F32)
    kq2 = mem.alloc("l0_kq2", [2, 80], F32)
    cbias = mem.alloc("l0_cb", [128, 2], F32)
    xr32 = mem.alloc("l0_xr32", [128, 2, 512], F32)

    def ldw(dst, src):
        P.dma(dst.v, src.rearranged("(c p) j -> p c j", p=128), queue="pool")

    ldw(w1, wk)
    ldw(w2, wv)

    def load_tabs(t0, par):
        P.dma(cosb[:, 0, :], G.cd["l0cos_b"][:, t0:t0 + 512], queue="sp")
        P.dma(sinb[:, 0, :], G.cd["l0sin_b"][:, t0:t0 + 512], queue="sp")

    for tc in range(8):
        t0 = 512 * tc
        src = View(xin.t.ap().rearrange("c p t -> p c t")[:, :, t0:t0 + 512],
                   xin.name, None)
        P.dma(xc.v, src, queue="pool" if xin_f32 else "sp")
        load_tabs(t0, tc % 2)
        ln_fm(P, G, lambda c: xc[:, c, :], lambda c: xc[:, c, :], 512)
        dst = View(xnsp.t.ap().rearrange("c p t -> p c t")[:, :, t0:t0 + 512],
                   xnsp.name, (t0, t0 + 512))
        P.dma(dst, xc.v, queue="sp")
        cosv = cosb[:, 0, :]
        sinv = sinb[:, 0, :]
        for p in range(8):
            kb = ps.bank(6)
            for c in range(8):
                P.mm(kb, w1[:, c, p * 128:(p + 1) * 128], xc[:, c, :],
                     start=(c == 0), stop=(c == 7))
            ktv = KT[:, p, t0:t0 + 512]
            rope_apply(P, G, kb, ps.bank(7), cosv, sinv, ktv)
            sq = G.sq[:, 0, :]
            P.act(sq, ktv, AF.Square)
            n2 = ps.bank(7, p0=0, p1=2)
            P.mm(n2, G.c["ones2_b"].v, sq, start=True, stop=True)
            P.reduce(kq2[:, p * 10 + tc: p * 10 + tc + 1], n2, ALU.max)
        for tb in range(4):
            ttg = tc * 4 + tb
            pv0 = ps.bank(0, c0=0, c1=512)
            pv1 = ps.bank(1, c0=0, c1=512)
            for c in range(8):
                stat = xc[:, c, tb * 128:(tb + 1) * 128]
                P.mm(pv0, stat, w2[:, c, 0:512], start=(c == 0), stop=(c == 7))
                P.mm(pv1, stat, w2[:, c, 512:1024], start=(c == 0), stop=(c == 7))
            srcv = View(ps.f32[0:128, 0:1024].rearrange("p (h e) -> p h e", e=64),
                        ps.name, (0, 4096))
            dstv = View(V.h[:, ttg, 0:1040].rearrange("p (h e) -> p h e", e=65)[:, :, 0:64],
                        V.name, (ttg * 1040 * 2, (ttg + 1) * 1040 * 2))
            P.copy(dstv, srcv)
    vones = View(V.h[:, :, :].rearrange("p t (h e) -> p t h e", e=65)[:, :, :, 64:65],
                 V.name, None)
    P.memset(vones, 1.0)

    ldw(w1, wq)
    ldw(w2, wo)
    for p in range(8):
        P.reduce(kq2[:, p * 10 + 8: p * 10 + 9],
                 kq2[:, p * 10: p * 10 + 8], ALU.max)

    for qb in range(8):
        q0 = 512 * qb
        src = View(xnsp.t.ap().rearrange("c p t -> p c t")[:, :, q0:q0 + 512],
                   xnsp.name, (q0, q0 + 512))
        P.dma(xc.v, src, queue="sp")
        load_tabs(q0, qb % 2)
        cosv = cosb[:, 0, :]
        sinv = sinb[:, 0, :]
        for p in range(8):
            qb6 = ps.bank(6)
            for c in range(8):
                P.mm(qb6, w1[:, c, p * 128:(p + 1) * 128], xc[:, c, :],
                     start=(c == 0), stop=(c == 7))
            rope_apply(P, G, qb6, ps.bank(7), cosv, sinv, qt.v)
            sq = G.sq[:, 0, :]
            P.act(sq, qt.v, AF.Square)
            n2 = ps.bank(7, p0=0, p1=2)
            P.mm(n2, G.c["ones2_b"].v, sq, start=True, stop=True)
            qm = kq2[:, p * 10 + 9: p * 10 + 10]
            P.reduce(qm, n2, ALU.max)
            prod = kq2[:, p * 10 + 8: p * 10 + 9]
            tmp21 = kq2[:, p * 10 + 7: p * 10 + 8]
            P.tt(tmp21, qm, prod, ALU.add)
            cb5 = ps.bank(5, c0=0, c1=2)
            P.mm(ps.bank(5, c0=0, c1=1), G.c["eneg2_f"].v, tmp21, start=True, stop=True)
            P.mm(ps.bank(5, c0=1, c1=2), G.c["eneg2b_f"].v, tmp21, start=True, stop=True)
            P.copy(cbias.v, cb5)
            for hh in range(2):
                rows = slice(64 * hh, 64 * hh + 64)
                nkc = 4 * (qb + 1)
                for gidx in range(qb + 1):
                    for j in range(4):
                        kc = 4 * gidx + j
                        scb = ps.bank(j)
                        P.mm(scb, KT[rows, p, kc * 128:(kc + 1) * 128], qt[rows, :],
                             start=True, stop=True)
                    P.act(pt.v, ps.banks(0, 4), AF.Exp, scale=0.125,
                          bias=cbias[:, hh:hh + 1])
                    if gidx == qb:
                        for j in range(4):
                            if j > 0:
                                P.memset(pt[:, j * 512: j * 512 + j * 128], 0.0)
                            dv = pt[:, j * 512 + j * 128: j * 512 + (j + 1) * 128]
                            P.tt(dv, dv, G.c["tri_b"].v, ALU.mult, engine="dve")
                    for j in range(4):
                        kc = 4 * gidx + j
                        P.mm(ps.bank(4, p0=0, p1=65),
                             V[:, kc, p * 130 + hh * 65: p * 130 + hh * 65 + 65],
                             pt[:, j * 512:(j + 1) * 512],
                             start=(kc == 0), stop=(kc == nkc - 1))
                z0 = G.rows[0:1, 0:512]
                z1 = G.rows[0:1, 512:1024]
                P.ts(z0, ps.bank(4, p0=64, p1=65), 1e-30, ALU.add)
                P.recip(z1, z0)
                P.mm(ps.bank(6, p0=0, p1=64), G.c["ones64_f"].v, z1,
                     start=True, stop=True)
                P.act(otmp.v, ps.bank(4, p0=0, p1=64), AF.Copy)
                P.tt(ot[rows, p, :], otmp.v, ps.bank(6, p0=0, p1=64), ALU.mult)
        for jc in range(8):
            P.dma(xr32[:, jc % 2, :], xin[jc, :, q0:q0 + 512], queue="sp")
            pj = ps.bank(jc % 2)
            for c in range(8):
                P.mm(pj, w2[:, c, jc * 128:(jc + 1) * 128], ot[:, c, :],
                     start=(c == 0), stop=(c == 7))
            rv = xr32[:, jc % 2, :]
            P.tt(rv, pj, rv, ALU.add)
            P.dma(View(xout.t.ap()[jc, :, q0:q0 + 512], xout.name,
                       (jc * 128 * 4096 + q0, jc * 128 * 4096 + q0 + 512)),
                  rv, queue="sp", output=True)
    mem.reset(m0)
    return {"l0_wqT": wq, "l0_wkT": wk, "l0_wvT": wv, "l0_woT": wo}


L0A_CONSTS = ["ones_col_b", "ones_row_f", "ones2_b", "eneg2_f", "eneg2b_f",
              "ones64_f", "rot_f", "l0cos_b", "l0sin_b", "tri_b", "eps_f"]


def make_l0a_program():
    nc = bass.Bass("TRN2", target_bir_lowering=False)
    P = Prog(nc)
    mem = Mem(nc)
    G = Globals(P, mem, L0A_CONSTS, hbm_only={"l0cos_b", "l0sin_b"})
    xin = DBuf(nc, "xin", [8, 128, 4096], F32, kind="ExternalInput")
    xout = DBuf(nc, "xout", [8, 128, 4096], F32, kind="ExternalOutput")
    build_l0_attn(P, G, mem, xin, xout, xin_f32=True)
    P.fence_outputs()
    P.emit()
    return nc, G



def host_prep_l0_ffn(inp):
    return {
        "l0f_w1T": np.ascontiguousarray(inp["L0_ffn_w1"].transpose(0, 2, 1)),
        "l0f_w2T": np.ascontiguousarray(inp["L0_ffn_w2"].transpose(0, 2, 1)),
        "l0f_gwT": np.ascontiguousarray(inp["L0_ffn_gate_w"].T),
        "l0f_gamma": np.ascontiguousarray(inp["L0_ffn_ln_gamma"].reshape(8, 128).T),
        "l0f_beta": np.ascontiguousarray(inp["L0_ffn_ln_beta"].reshape(8, 128).T),
    }


def build_l0_ffn(P, G, mem, xin, xout):
    nc = P.nc
    ps = G.psum
    w1d = DBuf(nc, "l0f_w1T", [8, 1024, 2048], F32, kind="ExternalInput")
    w2d = DBuf(nc, "l0f_w2T", [8, 2048, 1024], F32, kind="ExternalInput")
    gwd = DBuf(nc, "l0f_gwT", [1024, 8], F32, kind="ExternalInput")
    gmd = DBuf(nc, "l0f_gamma", [128, 8], F32, kind="ExternalInput")
    btd = DBuf(nc, "l0f_beta", [128, 8], F32, kind="ExternalInput")

    m0 = mem.mark()
    TC = 2048
    xacc = mem.alloc("l0f_xacc", [128, 8, TC], F32)
    xn = mem.alloc("l0f_xn", [128, 8, TC], BF16)
    w1b = mem.alloc("l0f_w1b", [128, 8, 2048], BF16)
    w2b = mem.alloc("l0f_w2b", [128, 16, 1024], BF16)
    hT = mem.alloc("l0f_hT", [128, 16, 512], BF16)
    combT = mem.alloc("l0f_combT", [8, TC], BF16)
    gwb = mem.alloc("l0f_gwb", [128, 8, 8], BF16)
    gam = mem.alloc("l0f_gam", [128, 8], F32)
    bet = mem.alloc("l0f_bet", [128, 8], F32)
    cbs = mem.alloc("l0f_cbs", [128, 512], BF16)
    lgr = mem.alloc("l0f_lgr", [8, 512], F32)
    gsc = mem.alloc("l0f_gsc", [128, 40], F32)

    P.dma(gwb.v, gwd.rearranged("(c p) e -> p c e", p=128), queue="pool")
    P.dma(gam.v, gmd.v, queue="sp")
    P.dma(bet.v, btd.v, queue="sp")
    idv = G.c["ident_f"]

    for TI in range(4096 // TC):
        T0 = TI * TC
        P.dma(xacc.v,
              View(xin.t.ap().rearrange("c p t -> p c t")[:, :, T0:T0 + TC],
                   xin.name, None), queue="sp")
        for ti in range(TC // 512):
            t0 = 512 * ti
            ln_fm(P, G, lambda c: xacc[:, c, t0:t0 + 512],
                  lambda c: xn[:, c, t0:t0 + 512], 512, in_f32=True)
            ln_fm(P, G, lambda c: xn[:, c, t0:t0 + 512],
                  lambda c: hT[:, c, :], 512,
                  gains=(lambda c: gam[:, c:c + 1], lambda c: bet[:, c:c + 1]))
            lg = ps.bank(6, p0=0, p1=8)
            for c in range(8):
                P.mm(lg, gwb[:, c, :], hT[:, c, :], start=(c == 0), stop=(c == 7))
            P.copy(lgr.v, lg)
            for g4 in range(4):
                tg = ps.bank(7, p0=0, p1=128, c0=g4 * 8, c1=g4 * 8 + 8)
                P.transpose(tg, lgr[:, g4 * 128:(g4 + 1) * 128], idv[0:8, 0:8])
                nm = gsc[:, 24:25]
                P.reduce(nm, tg, ALU.max, negate=True)
                pr = gsc[:, 0:8]
                P.act(pr, tg, AF.Exp, bias=nm)
                sm = gsc[:, 25:26]
                P.reduce(sm, pr, ALU.add)
                rc = gsc[:, 26:27]
                P.recip(rc, sm)
                P.ts(pr, pr, rc, ALU.mult)
                m1 = gsc[:, 27:28]
                P.reduce(m1, pr, ALU.max)
                ge = gsc[:, 8:16]
                P.ts(ge, pr, m1, ALU.is_ge)
                msk = gsc[:, 16:24]
                P.stt(msk, ge, -1e30, pr, ALU.mult, ALU.add)
                m2 = gsc[:, 28:29]
                P.reduce(m2, msk, ALU.max)
                P.ts(ge, pr, m2, ALU.is_ge)
                cmb = gsc[:, 16:24]
                P.tt(cmb, ge, pr, ALU.mult)
                ct = ps.bank(6, p0=0, p1=8, c0=0, c1=128)
                P.transpose(ct, cmb, idv[:, 0:128])
                P.copy(combT[:, t0 + g4 * 128: t0 + (g4 + 1) * 128], ct)

        for e in range(8):
            P.dma(w1b.v, View(
                w1d.t.ap()[e].rearrange("(c p) f -> p c f", p=128), w1d.name, None),
                queue="pool")
            P.dma(w2b.v, View(
                w2d.t.ap()[e].rearrange("(c p) j -> p c j", p=128), w2d.name, None),
                queue="pool")
            for ti in range(TC // 512):
                t0 = 512 * ti
                cb = ps.bank(6)
                P.mm(cb, G.c["esel8_b"][:, e * 128:(e + 1) * 128],
                     combT[:, t0:t0 + 512], start=True, stop=True)
                P.copy(cbs.v, cb)
                for hc in range(16):
                    pb = ps.bank(hc % 2)
                    for c in range(8):
                        P.mm(pb, w1b[:, c, hc * 128:(hc + 1) * 128],
                             xn[:, c, t0:t0 + 512], start=(c == 0), stop=(c == 7))
                    hv = hT[:, hc, :]
                    P.act(hv, pb, AF.Gelu)
                    P.tt(hv, hv, cbs.v, ALU.mult,
                         engine="dve")
                for cc in range(8):
                    pb = ps.bank(2 + cc % 2)
                    for f in range(16):
                        P.mm(pb, w2b[:, f, cc * 128:(cc + 1) * 128], hT[:, f, :],
                             start=(f == 0), stop=(f == 15))
                    xv = xacc[:, cc, t0:t0 + 512]
                    P.tt(xv, pb, xv, ALU.add)
        P.dma(View(xout.t.ap().rearrange("c p t -> p c t")[:, :, T0:T0 + TC],
                   xout.name, (T0, T0 + TC)),
              xacc.v, queue="sp", output=True)
    mem.reset(m0)


L0F_CONSTS = ["ones_col_b", "ones_col_f", "ones_row_f", "ones_row_b", "ident_f",
              "esel8_b", "eps_f"]


def make_l0f_program():
    nc = bass.Bass("TRN2", target_bir_lowering=False)
    P = Prog(nc)
    mem = Mem(nc)
    G = Globals(P, mem, L0F_CONSTS)
    xin = DBuf(nc, "xin", [8, 128, 4096], F32, kind="ExternalInput")
    xout = DBuf(nc, "xout", [8, 128, 4096], F32, kind="ExternalOutput")
    build_l0_ffn(P, G, mem, xin, xout)
    P.fence_outputs()
    P.emit()
    return nc, G



def host_prep_l1_attn(inp):
    return {
        "l1_wq": np.ascontiguousarray(inp["L1_attn_wq_w"]),
        "l1_wk": np.ascontiguousarray(inp["L1_attn_wk_w"]),
        "l1_wv": np.ascontiguousarray(inp["L1_attn_wv_w"]),
        "l1_wg": np.ascontiguousarray(inp["L1_attn_wg_w"]),
        "l1_wo": np.ascontiguousarray(inp["L1_attn_out_w"]),
        "l1_qb": np.ascontiguousarray(inp["L1_attn_wq_b"].reshape(8, 128).T),
        "l1_gb": np.ascontiguousarray(inp["L1_attn_wg_b"].reshape(8, 128).T),
        "l1_ob": np.ascontiguousarray(inp["L1_attn_out_b"].reshape(8, 128).T),
        "l1_kb": np.ascontiguousarray(inp["L1_attn_wk_b"].reshape(1, 1024)),
        "l1_vb": np.ascontiguousarray(inp["L1_attn_wv_b"].reshape(1, 1024)),
    }


def build_l1_attn(P, G, mem, xin, xout):
    nc = P.nc
    ps = G.psum
    wqd = DBuf(nc, "l1_wq", [1024, 1024], F32, kind="ExternalInput")
    wkd = DBuf(nc, "l1_wk", [1024, 1024], F32, kind="ExternalInput")
    wvd = DBuf(nc, "l1_wv", [1024, 1024], F32, kind="ExternalInput")
    wgd = DBuf(nc, "l1_wg", [1024, 1024], F32, kind="ExternalInput")
    wod = DBuf(nc, "l1_wo", [1024, 1024], F32, kind="ExternalInput")
    qbd = DBuf(nc, "l1_qb", [128, 8], F32, kind="ExternalInput")
    gbd = DBuf(nc, "l1_gb", [128, 8], F32, kind="ExternalInput")
    obd = DBuf(nc, "l1_ob", [128, 8], F32, kind="ExternalInput")
    kbd = DBuf(nc, "l1_kb", [1, 1024], F32, kind="ExternalInput")
    vbd = DBuf(nc, "l1_vb", [1, 1024], F32, kind="ExternalInput")

    m0 = mem.mark()
    xn = mem.alloc("l1_xn", [128, 8, 4096], BF16)
    w1 = mem.alloc("l1_w1", [128, 8, 1024], BF16)
    w2 = mem.alloc("l1_w2", [128, 8, 1024], BF16)
    w3 = mem.alloc("l1_w3", [128, 8, 1024], BF16)
    vt = mem.alloc("l1_vt", [128, 2, 1024], BF16)
    kt = mem.alloc("l1_kt", [128, 2, 1024], BF16)
    kvacc = mem.alloc("l1_kvacc", [128, 4, 128], F32)
    kvs = mem.alloc("l1_kvs", [128, 4, 128], BF16)
    kstat = mem.alloc("l1_kstat", [128, 2, 8], BF16)
    gateT = mem.alloc("l1_gateT", [128, 8, 512], BF16)
    gsT = mem.alloc("l1_gsT", [128, 8, 512], BF16)
    qb_ = mem.alloc("l1_qbb", [128, 8], F32)
    gb_ = mem.alloc("l1_gbb", [128, 8], F32)
    ob_ = mem.alloc("l1_obb", [128, 8], F32)
    kb_ = mem.alloc("l1_kbb", [1, 1024], BF16)
    vb_ = mem.alloc("l1_vbb", [1, 1024], BF16)
    xc = mem.alloc("l1_xc", [128, 8, 512], BF16)
    xr32 = mem.alloc("l1_xr32", [128, 2, 512], F32)
    krow = mem.alloc("l1_krow", [1, 1024], F32)
    zrec = mem.alloc("l1_zrec", [2, 512], F32)
    phq = mem.alloc("l1_phq", [128, 512], BF16)
    sctmp = mem.alloc("l1_sctmp", [128, 512], F32)

    def ldw(dst, src):
        P.dma(dst.v, src.rearranged("(c p) j -> p c j", p=128), queue="pool")

    ldw(w1, wkd)
    ldw(w2, wvd)
    P.dma(qb_.v, qbd.v, queue="sp")
    P.dma(gb_.v, gbd.v, queue="sp")
    P.dma(ob_.v, obd.v, queue="sp")
    P.dma(kb_.v, kbd.v, queue="pool")
    P.dma(vb_.v, vbd.v, queue="pool")

    def phi(dst, src_psum, W):
        t1 = View(G.tmpa.h[:, :, :].rearrange("p a b -> p (a b)")[:, 0:W],
                  G.tmpa.name, (0, W))
        t2 = View(G.tmpb.h[:, :, :].rearrange("p a b -> p (a b)")[:, 0:W],
                  G.tmpb.name, (0, W))
        P.ts(t1, src_psum, 0.0, ALU.min)
        P.act(t1, t1, AF.Exp)
        P.ts(t2, src_psum, 0.0, ALU.max)
        P.tt(dst, t1, t2, ALU.add)

    for tc in range(8):
        t0 = 512 * tc
        P.dma(xc.v, View(xin.t.ap().rearrange("c p t -> p c t")[:, :, t0:t0 + 512],
                         xin.name, None), queue="pool")
        ln_fm(P, G, lambda c: xc[:, c, :], lambda c: xn[:, c, t0 + 0:t0 + 512], 512)
        for tb in range(4):
            par = tb % 2
            tglob = tc * 4 + tb
            kp = [ps.bank(6), ps.bank(7)]
            for half in range(2):
                P.mm(kp[half], G.c["ones_row_b"].v, kb_[0:1, half * 512:(half + 1) * 512],
                     start=True, stop=False)
                for c in range(8):
                    P.mm(kp[half], xn[:, c, t0 + tb * 128:t0 + (tb + 1) * 128],
                         w1[:, c, half * 512:(half + 1) * 512],
                         start=False, stop=(c == 7))
            phi(kt[:, par, :], ps.banks(6, 2), 1024)
            vp = [ps.bank(6), ps.bank(7)]
            for half in range(2):
                P.mm(vp[half], G.c["ones_row_b"].v, vb_[0:1, half * 512:(half + 1) * 512],
                     start=True, stop=False)
                for c in range(8):
                    P.mm(vp[half], xn[:, c, t0 + tb * 128:t0 + (tb + 1) * 128],
                         w2[:, c, half * 512:(half + 1) * 512],
                         start=False, stop=(c == 7))
            P.copy(vt[:, par, :], ps.banks(6, 2))
            first = (tb == 0)
            last = (tb == 3)
            for h in range(16):
                b = h // 4
                r0 = 64 * (h % 2)
                c0 = 64 * ((h % 4) // 2)
                P.mm(ps.bank(b, p0=r0, p1=r0 + 64, c0=c0, c1=c0 + 64),
                     kt[:, par, h * 64:(h + 1) * 64], vt[:, par, h * 64:(h + 1) * 64],
                     start=first, stop=last)
            for half in range(2):
                P.mm(ps.bank(4 + half, p0=0, p1=1, c0=0, c1=512),
                     G.c["ones_col_b"].v, kt[:, par, half * 512:(half + 1) * 512],
                     start=first, stop=last)
        for b in range(4):
            if tc == 0:
                P.copy(kvacc[:, b, :], ps.bank(b, c0=0, c1=128))
            else:
                P.tt(kvacc[:, b, :], ps.bank(b, c0=0, c1=128), kvacc[:, b, :], ALU.add)
        for half in range(2):
            kr = krow[0:1, half * 512:(half + 1) * 512]
            if tc == 0:
                P.copy(kr, ps.bank(4 + half, p0=0, p1=1, c0=0, c1=512))
            else:
                P.tt(kr, ps.bank(4 + half, p0=0, p1=1, c0=0, c1=512), kr, ALU.add)
    for b in range(4):
        P.copy(kvs[:, b, :], kvacc[:, b, :])
    for p in range(8):
        kcT = ps.bank(2, p0=0, p1=128, c0=p * 4, c1=p * 4 + 1)
        P.transpose(kcT, krow[0:1, p * 128:(p + 1) * 128], G.c["ident_f"][0:1, 0:1])
        P.tt(kstat[:, p // 4, (p % 4) * 2: (p % 4) * 2 + 1], kcT,
             G.c["mh0_f"].v, ALU.mult)
        P.tt(kstat[:, p // 4, (p % 4) * 2 + 1: (p % 4) * 2 + 2], kcT,
             G.c["mh1_f"].v, ALU.mult)

    ldw(w1, wqd)
    ldw(w2, wod)
    ldw(w3, wgd)
    for tc in range(8):
        t0 = 512 * tc
        for jc in range(8):
            gp = ps.bank(7)
            for c in range(8):
                P.mm(gp, w3[:, c, jc * 128:(jc + 1) * 128], xn[:, c, t0:t0 + 512],
                     start=(c == 0), stop=(c == 7))
            P.act(gateT[:, jc, :], gp, AF.Sigmoid, bias=gb_[:, jc:jc + 1])
        for p in range(8):
            qp = ps.bank(6)
            for c in range(8):
                P.mm(qp, w1[:, c, p * 128:(p + 1) * 128], xn[:, c, t0:t0 + 512],
                     start=(c == 0), stop=(c == 7))
            t1 = View(G.tmpa.h[:, :, :].rearrange("p a b -> p (a b)")[:, 0:512],
                      G.tmpa.name, (0, 512))
            t2 = View(G.tmpb.h[:, :, :].rearrange("p a b -> p (a b)")[:, 0:512],
                      G.tmpb.name, (0, 512))
            P.ts(t1, qp, qb_[:, p:p + 1], ALU.add, 0.0, ALU.min)
            P.act(t1, t1, AF.Exp)
            P.ts(t2, qp, qb_[:, p:p + 1], ALU.add, 0.0, ALU.max)
            P.tt(phq.v, t1, t2, ALU.add)
            zp = ps.bank(6, p0=0, p1=2)
            P.mm(zp, kstat[:, p // 4, (p % 4) * 2:(p % 4) * 2 + 2], phq.v,
                 start=True, stop=True)
            P.act(zrec.v, zp, AF.Ln, bias=G.c["eps_f"][0:2, 1:2])
            P.act(zrec.v, zrec.v, AF.Exp, scale=-1.0)
            for hh in range(2):
                h = 2 * p + hh
                r0 = 64 * (h % 2)
                c0 = 64 * ((h % 4) // 2)
                P.mm(ps.bank(4, p0=64 * hh, p1=64 * hh + 64),
                     kvs[r0:r0 + 64, h // 4, c0:c0 + 64],
                     phq[64 * hh:64 * hh + 64, :],
                     start=True, stop=True)
            P.mm(ps.bank(5), G.c["esel2_f"].v, zrec.v, start=True, stop=True)
            P.act(sctmp.v, ps.bank(4), AF.Copy)
            P.tt(sctmp.v, sctmp.v, ps.bank(5), ALU.mult)
            P.tt(gsT[:, p, :], sctmp.v, gateT[:, p, :], ALU.mult)
        for jc in range(8):
            P.dma(xr32[:, jc % 2, :], xin[jc, :, t0:t0 + 512], queue="sp")
            pj = ps.bank(jc % 2)
            for c in range(8):
                P.mm(pj, w2[:, c, jc * 128:(jc + 1) * 128], gsT[:, c, :],
                     start=(c == 0), stop=(c == 7))
            rv = xr32[:, jc % 2, :]
            P.stt(rv, pj, ob_[:, jc:jc + 1], rv, ALU.add, ALU.add)
            P.dma(View(xout.t.ap()[jc, :, t0:t0 + 512], xout.name,
                       (jc * 128 * 4096 + t0, jc * 128 * 4096 + t0 + 512)),
                  rv, queue="sp", output=True)
    mem.reset(m0)


L1A_CONSTS = ["ones_col_b", "ones_row_b", "ones_row_f", "ones_col_f", "ident_f",
              "mh0_f", "mh1_f", "esel2_f", "eps_f"]


def make_l1a_program():
    nc = bass.Bass("TRN2", target_bir_lowering=False)
    P = Prog(nc)
    mem = Mem(nc)
    G = Globals(P, mem, L1A_CONSTS)
    xin = DBuf(nc, "xin", [8, 128, 4096], F32, kind="ExternalInput")
    xout = DBuf(nc, "xout", [8, 128, 4096], F32, kind="ExternalOutput")
    build_l1_attn(P, G, mem, xin, xout)
    P.fence_outputs()
    P.emit()
    return nc, G



def host_prep_l1_ffn(inp):
    return {
        "l1f_wgT": np.ascontiguousarray(inp["L1_ffn_w_gate"].T),
        "l1f_wuT": np.ascontiguousarray(inp["L1_ffn_w_up"].T),
        "l1f_wdT": np.ascontiguousarray(inp["L1_ffn_w_down"].T),
    }


def build_l1_ffn(P, G, mem, xin, xout):
    nc = P.nc
    ps = G.psum
    wgd = DBuf(nc, "l1f_wgT", [1024, 3072], I32, kind="ExternalInput")
    wud = DBuf(nc, "l1f_wuT", [1024, 3072], I32, kind="ExternalInput")
    wdd = DBuf(nc, "l1f_wdT", [3072, 1024], I32, kind="ExternalInput")

    m0 = mem.mark()
    wg = mem.alloc("l1f_wg", [128, 8, 3072], BF16)
    wu = mem.alloc("l1f_wu", [128, 8, 3072], BF16)
    wd = mem.alloc("l1f_wd", [128, 24, 1024], BF16)
    hT = mem.alloc("l1f_hT", [128, 24, 512], BF16)
    sg = mem.alloc("l1f_sg", [128, 512], BF16)
    xc = mem.alloc("l1f_xc", [128, 8, 512], BF16)
    xr32 = mem.alloc("l1f_xr32", [128, 2, 512], F32)

    P.dma(wg.v, wgd.rearranged("(c p) j -> p c j", p=128), queue="pool")
    P.dma(wu.v, wud.rearranged("(c p) j -> p c j", p=128), queue="pool")
    P.dma(wd.v, wdd.rearranged("(c p) j -> p c j", p=128), queue="pool")

    for ti in range(8):
        t0 = 512 * ti
        P.dma(xc.v, View(xin.t.ap().rearrange("c p t -> p c t")[:, :, t0:t0 + 512],
                         xin.name, None), queue="pool")
        ln_fm(P, G, lambda c: xc[:, c, :], lambda c: xc[:, c, :], 512, in_f32=False)
        for hc in range(24):
            gp = ps.bank(hc % 2)
            up = ps.bank(2 + hc % 2)
            for c in range(8):
                P.mm(gp, wg[:, c, hc * 128:(hc + 1) * 128], xc[:, c, :],
                     start=(c == 0), stop=(c == 7))
            for c in range(8):
                P.mm(up, wu[:, c, hc * 128:(hc + 1) * 128], xc[:, c, :],
                     start=(c == 0), stop=(c == 7))
            P.act(sg.v, gp, AF.Silu, scale=1.0 / 127.0)
            P.tt(hT[:, hc, :], sg.v, up, ALU.mult)
        for jc in range(8):
            P.dma(xr32[:, jc % 2, :], xin[jc, :, t0:t0 + 512], queue="sp")
            dp = ps.bank(4 + jc % 2)
            for f in range(24):
                P.mm(dp, wd[:, f, jc * 128:(jc + 1) * 128], hT[:, f, :],
                     start=(f == 0), stop=(f == 23))
            rv = xr32[:, jc % 2, :]
            P.stt(rv, dp, 1.0 / 16129.0, rv, ALU.mult, ALU.add)
            P.dma(View(xout.t.ap()[jc, :, t0:t0 + 512], xout.name,
                       (jc * 128 * 4096 + t0, jc * 128 * 4096 + t0 + 512)),
                  rv, queue="sp", output=True)
    mem.reset(m0)


L1F_CONSTS = ["ones_col_b", "ones_col_f", "ones_row_f", "eps_f"]


def make_l1f_program():
    nc = bass.Bass("TRN2", target_bir_lowering=False)
    P = Prog(nc)
    mem = Mem(nc)
    G = Globals(P, mem, L1F_CONSTS)
    xin = DBuf(nc, "xin", [8, 128, 4096], F32, kind="ExternalInput")
    xout = DBuf(nc, "xout", [8, 128, 4096], F32, kind="ExternalOutput")
    build_l1_ffn(P, G, mem, xin, xout)
    P.fence_outputs()
    P.emit()
    return nc, G



def host_prep_l2_attn(inp):
    return {
        "l2_wq": np.ascontiguousarray(inp["L2_attn_Wq"]),
        "l2_wk": np.ascontiguousarray(inp["L2_attn_Wk"]),
        "l2_wv": np.ascontiguousarray(inp["L2_attn_Wv"]),
        "l2_wo": np.ascontiguousarray(inp["L2_attn_Wo"]),
        "l2_bo": np.ascontiguousarray(inp["L2_attn_bo"].reshape(8, 128).T),
    }


def build_l2_attn(P, G, mem, xin, xout):
    nc = P.nc
    ps = G.psum
    wqd = DBuf(nc, "l2_wq", [1024, 1024], F32, kind="ExternalInput")
    wkd = DBuf(nc, "l2_wk", [1024, 1024], F32, kind="ExternalInput")
    wvd = DBuf(nc, "l2_wv", [1024, 1024], F32, kind="ExternalInput")
    wod = DBuf(nc, "l2_wo", [1024, 1024], F32, kind="ExternalInput")
    bod = DBuf(nc, "l2_bo", [128, 8], F32, kind="ExternalInput")

    m0 = mem.mark()
    xn = mem.alloc("l2_xn", [128, 8, 4096], BF16)
    w1 = mem.alloc("l2_w1", [128, 8, 1024], BF16)
    w2 = mem.alloc("l2_w2", [128, 8, 1024], BF16)
    vt = mem.alloc("l2_vt", [128, 2, 1024], BF16)
    kt = mem.alloc("l2_kt", [128, 2, 1024], BF16)
    kvacc = mem.alloc("l2_kvacc", [128, 4, 128], F32)
    kvs = mem.alloc("l2_kvs", [128, 4, 128], BF16)
    kstat = mem.alloc("l2_kstat", [128, 2, 8], BF16)
    gsT = mem.alloc("l2_gsT", [128, 8, 512], BF16)
    bo_ = mem.alloc("l2_bob", [128, 8], F32)
    xc = mem.alloc("l2_xc", [128, 8, 512], BF16)
    xr32 = mem.alloc("l2_xr32", [128, 2, 512], F32)
    krow = mem.alloc("l2_krow", [1, 1024], F32)
    zrec = mem.alloc("l2_zrec", [2, 512], F32)
    phq = mem.alloc("l2_phq", [128, 512], BF16)
    sctmp = mem.alloc("l2_sctmp", [128, 512], F32)

    def ldw(dst, src):
        P.dma(dst.v, src.rearranged("(c p) j -> p c j", p=128), queue="pool")

    ldw(w1, wkd)
    ldw(w2, wvd)
    P.dma(bo_.v, bod.v, queue="sp")

    def phi(dst, src_psum, W):
        t1 = View(G.tmpa.h[:, :, :].rearrange("p a b -> p (a b)")[:, 0:W],
                  G.tmpa.name, (0, W))
        t2 = View(G.tmpb.h[:, :, :].rearrange("p a b -> p (a b)")[:, 0:W],
                  G.tmpb.name, (0, W))
        P.ts(t1, src_psum, 0.0, ALU.min)
        P.act(t1, t1, AF.Exp)
        P.ts(t2, src_psum, 0.0, ALU.max, 1.0, ALU.add)
        P.tt(dst, t1, t2, ALU.add)

    def phi2(dst, src_psum, W):
        t1 = View(G.tmpa.h[:, :, :].rearrange("p a b -> p (a b)")[:, 0:W],
                  G.tmpa.name, (0, W))
        t2 = View(G.tmpb.h[:, :, :].rearrange("p a b -> p (a b)")[:, 0:W],
                  G.tmpb.name, (0, W))
        P.ts(t1, src_psum, 0.0, ALU.min)
        P.act(t1, t1, AF.Exp)
        P.ts(t2, src_psum, 0.0, ALU.max)
        P.tt(dst, t1, t2, ALU.add)

    for tc in range(8):
        t0 = 512 * tc
        P.dma(xc.v, View(xin.t.ap().rearrange("c p t -> p c t")[:, :, t0:t0 + 512],
                         xin.name, None), queue="pool")
        ln_fm(P, G, lambda c: xc[:, c, :], lambda c: xn[:, c, t0:t0 + 512], 512)
        for tb in range(4):
            par = tb % 2
            kp = [ps.bank(6), ps.bank(7)]
            for half in range(2):
                for c in range(8):
                    P.mm(kp[half], xn[:, c, t0 + tb * 128:t0 + (tb + 1) * 128],
                         w1[:, c, half * 512:(half + 1) * 512],
                         start=(c == 0), stop=(c == 7))
            phi2(kt[:, par, :], ps.banks(6, 2), 1024)
            vp = [ps.bank(6), ps.bank(7)]
            for half in range(2):
                for c in range(8):
                    P.mm(vp[half], xn[:, c, t0 + tb * 128:t0 + (tb + 1) * 128],
                         w2[:, c, half * 512:(half + 1) * 512],
                         start=(c == 0), stop=(c == 7))
            P.copy(vt[:, par, :], ps.banks(6, 2))
            first = (tb == 0)
            last = (tb == 3)
            for h in range(16):
                b = h // 4
                r0 = 64 * (h % 2)
                c0 = 64 * ((h % 4) // 2)
                P.mm(ps.bank(b, p0=r0, p1=r0 + 64, c0=c0, c1=c0 + 64),
                     kt[:, par, h * 64:(h + 1) * 64], vt[:, par, h * 64:(h + 1) * 64],
                     start=first, stop=last)
            for half in range(2):
                P.mm(ps.bank(4 + half, p0=0, p1=1, c0=0, c1=512),
                     G.c["ones_col_b"].v, kt[:, par, half * 512:(half + 1) * 512],
                     start=first, stop=last)
        for b in range(4):
            if tc == 0:
                P.copy(kvacc[:, b, :], ps.bank(b, c0=0, c1=128))
            else:
                P.tt(kvacc[:, b, :], ps.bank(b, c0=0, c1=128), kvacc[:, b, :], ALU.add)
        for half in range(2):
            kr = krow[0:1, half * 512:(half + 1) * 512]
            if tc == 0:
                P.copy(kr, ps.bank(4 + half, p0=0, p1=1, c0=0, c1=512))
            else:
                P.tt(kr, ps.bank(4 + half, p0=0, p1=1, c0=0, c1=512), kr, ALU.add)
    for b in range(4):
        P.copy(kvs[:, b, :], kvacc[:, b, :])
    for p in range(8):
        kcT = ps.bank(2, p0=0, p1=128, c0=p * 4, c1=p * 4 + 1)
        P.transpose(kcT, krow[0:1, p * 128:(p + 1) * 128], G.c["ident_f"][0:1, 0:1])
        P.tt(kstat[:, p // 4, (p % 4) * 2: (p % 4) * 2 + 1], kcT,
             G.c["mh0_f"].v, ALU.mult)
        P.tt(kstat[:, p // 4, (p % 4) * 2 + 1: (p % 4) * 2 + 2], kcT,
             G.c["mh1_f"].v, ALU.mult)

    ldw(w1, wqd)
    ldw(w2, wod)
    for tc in range(8):
        t0 = 512 * tc
        for p in range(8):
            qp = ps.bank(6)
            for c in range(8):
                P.mm(qp, w1[:, c, p * 128:(p + 1) * 128], xn[:, c, t0:t0 + 512],
                     start=(c == 0), stop=(c == 7))
            phi2(phq.v, qp, 512)
            zp = ps.bank(6, p0=0, p1=2)
            P.mm(zp, kstat[:, p // 4, (p % 4) * 2:(p % 4) * 2 + 2], phq.v,
                 start=True, stop=True)
            P.act(zrec.v, zp, AF.Ln, bias=G.c["eps_f"][0:2, 1:2])
            P.act(zrec.v, zrec.v, AF.Exp, scale=-1.0)
            for hh in range(2):
                h = 2 * p + hh
                r0 = 64 * (h % 2)
                c0 = 64 * ((h % 4) // 2)
                P.mm(ps.bank(4, p0=64 * hh, p1=64 * hh + 64),
                     kvs[r0:r0 + 64, h // 4, c0:c0 + 64],
                     phq[64 * hh:64 * hh + 64, :],
                     start=True, stop=True)
            P.mm(ps.bank(5), G.c["esel2_f"].v, zrec.v, start=True, stop=True)
            P.act(sctmp.v, ps.bank(4), AF.Copy)
            P.tt(gsT[:, p, :], sctmp.v, ps.bank(5), ALU.mult)
        for jc in range(8):
            P.dma(xr32[:, jc % 2, :], xin[jc, :, t0:t0 + 512], queue="sp")
            pj = ps.bank(jc % 2)
            for c in range(8):
                P.mm(pj, w2[:, c, jc * 128:(jc + 1) * 128], gsT[:, c, :],
                     start=(c == 0), stop=(c == 7))
            rv = xr32[:, jc % 2, :]
            P.stt(rv, pj, bo_[:, jc:jc + 1], rv, ALU.add, ALU.add)
            P.dma(View(xout.t.ap()[jc, :, t0:t0 + 512], xout.name,
                       (jc * 128 * 4096 + t0, jc * 128 * 4096 + t0 + 512)),
                  rv, queue="sp", output=True)
    mem.reset(m0)


L2A_CONSTS = ["ones_col_b", "ones_col_f", "ones_row_f", "ident_f",
              "mh0_f", "mh1_f", "esel2_f", "eps_f"]


def make_l2a_program():
    nc = bass.Bass("TRN2", target_bir_lowering=False)
    P = Prog(nc)
    mem = Mem(nc)
    G = Globals(P, mem, L2A_CONSTS)
    xin = DBuf(nc, "xin", [8, 128, 4096], F32, kind="ExternalInput")
    xout = DBuf(nc, "xout", [8, 128, 4096], F32, kind="ExternalOutput")
    build_l2_attn(P, G, mem, xin, xout)
    P.fence_outputs()
    P.emit()
    return nc, G



def host_prep_inner(inp, pfx, key, depth, heads, window):
    inner = heads * 64
    if pfx == "l2f":
        wqkv = inp["L2_ffn_Wqkv"]; qs = inp["L2_ffn_q_scale"]; ks = inp["L2_ffn_k_scale"]
        wo = inp["L2_ffn_Wo"]; lng = inp["L2_ffn_ln_g"]; lnb = inp["L2_ffn_ln_b"]
        w1 = inp["L2_ffn_W1"]; b1 = inp["L2_ffn_b1"]; w2 = inp["L2_ffn_W2"]; b2 = inp["L2_ffn_b2"]
    else:
        wqkv = inp["L3_ffn_wqkv"]; qs = inp["L3_ffn_q_scale"]; ks = inp["L3_ffn_k_scale"]
        wo = inp["L3_ffn_wo"]; lng = inp["L3_ffn_ln_g"]; lnb = inp["L3_ffn_ln_b"]
        w1 = inp["L3_ffn_w1"]; b1 = inp["L3_ffn_b1"]; w2 = inp["L3_ffn_w2"]; b2 = inp["L3_ffn_b2"]
    d = {}
    d[pfx + "_wqkv"] = np.ascontiguousarray(wqkv)
    d[pfx + "_wo"] = np.ascontiguousarray(wo)
    a = w1[:, :, :2730]; g = w1[:, :, 2730:]
    pad = np.zeros((depth, 1024, 86), w1.dtype)
    d[pfx + "_w1a"] = np.ascontiguousarray(np.concatenate([a, pad], axis=2))
    d[pfx + "_w1g"] = np.ascontiguousarray(np.concatenate([g, pad], axis=2))
    b1a = np.concatenate([b1[:, :2730], np.zeros((depth, 86), b1.dtype)], axis=1)
    b1g = np.concatenate([b1[:, 2730:], np.zeros((depth, 86), b1.dtype)], axis=1)
    d[pfx + "_b1a"] = np.ascontiguousarray(b1a.reshape(depth, 22, 128).transpose(0, 2, 1))
    d[pfx + "_b1g"] = np.ascontiguousarray(b1g.reshape(depth, 22, 128).transpose(0, 2, 1))
    w2p = np.concatenate([w2, np.zeros((depth, 86, 1024), w2.dtype)], axis=1)
    d[pfx + "_w2"] = np.ascontiguousarray(w2p)
    d[pfx + "_b2"] = np.ascontiguousarray(b2.reshape(depth, 8, 128).transpose(0, 2, 1))
    d[pfx + "_lng"] = np.ascontiguousarray(lng.reshape(depth, 8, 128).transpose(0, 2, 1))
    d[pfx + "_lnb"] = np.ascontiguousarray(lnb.reshape(depth, 8, 128).transpose(0, 2, 1))
    qst = np.zeros((depth, 2, 128), np.float32)
    kst = np.zeros((depth, 2, 128), np.float32)
    for i in range(depth):
        qst[i, 0, 0:64] = qs[i]; qst[i, 1, 64:128] = qs[i]
        kst[i, 0, 0:64] = ks[i]; kst[i, 1, 64:128] = ks[i]
    d[pfx + "_qst"] = qst
    d[pfx + "_kst"] = kst
    half = np.arange(0, 64, 2, dtype=np.float64)
    inv_freq = 1.0 / 10000.0 ** (half / 64.0)
    t = np.arange(4096, dtype=np.float64)
    freqs = t[:, None] * inv_freq[None, :]
    freqs = np.concatenate([freqs, freqs], -1)
    base = (half + 0.4 * 64) / (1.4 * 64)
    power = (t - 4096 // 2) / (window // 2)
    scale = base[None, :] ** power[:, None]
    scale = np.concatenate([scale, scale], -1)
    cos, sin = np.cos(freqs), np.sin(freqs)
    def dup(x):
        x = x.T.astype(np.float32)
        return bf16(np.concatenate([x, x], axis=0))
    d[pfx + "_qcos"] = dup(cos * scale)
    d[pfx + "_qsin"] = dup(sin * scale)
    d[pfx + "_kcos"] = dup(cos / scale)
    d[pfx + "_ksin"] = dup(sin / scale)
    return d


def build_inner_tf(P, G, mem, xin, xout, pfx, depth, heads, window):
    nc = P.nc
    ps = G.psum
    pairs = heads // 2
    inner = heads * 64
    nb = window // 128
    ring = nb + 4

    wqkvd = DBuf(nc, pfx + "_wqkv", [depth, 1024, 3 * inner], F32, kind="ExternalInput")
    wod = DBuf(nc, pfx + "_wo", [depth, inner, 1024], F32, kind="ExternalInput")
    w1ad = DBuf(nc, pfx + "_w1a", [depth, 1024, 2816], F32, kind="ExternalInput")
    w1gd = DBuf(nc, pfx + "_w1g", [depth, 1024, 2816], F32, kind="ExternalInput")
    w2d = DBuf(nc, pfx + "_w2", [depth, 2816, 1024], F32, kind="ExternalInput")
    b1ad = DBuf(nc, pfx + "_b1a", [depth, 128, 22], F32, kind="ExternalInput")
    b1gd = DBuf(nc, pfx + "_b1g", [depth, 128, 22], F32, kind="ExternalInput")
    b2d = DBuf(nc, pfx + "_b2", [depth, 128, 8], F32, kind="ExternalInput")
    lngd = DBuf(nc, pfx + "_lng", [depth, 128, 8], F32, kind="ExternalInput")
    lnbd = DBuf(nc, pfx + "_lnb", [depth, 128, 8], F32, kind="ExternalInput")
    qstd = DBuf(nc, pfx + "_qst", [depth, 2, 128], F32, kind="ExternalInput")
    kstd = DBuf(nc, pfx + "_kst", [depth, 2, 128], F32, kind="ExternalInput")
    tabd = {n: DBuf(nc, pfx + "_" + n, [128, 4096], BF16, kind="ExternalInput")
            for n in ("qcos", "qsin", "kcos", "ksin")}

    m0 = mem.mark()
    y = mem.alloc(pfx + "_y", [128, 8, 4104], BF16)
    tabs = {n: mem.alloc(pfx + "_t" + n, [128, 512], BF16) for n in tabd}
    qst = mem.alloc(pfx + "_qstb", [2, 128], F32)
    kst = mem.alloc(pfx + "_kstb", [2, 128], F32)
    lng = mem.alloc(pfx + "_lngb", [128, 8], F32)
    lnb = mem.alloc(pfx + "_lnbb", [128, 8], F32)
    b1a = mem.alloc(pfx + "_b1ab", [128, 22], F32)
    b1g = mem.alloc(pfx + "_b1gb", [128, 22], F32)
    b2 = mem.alloc(pfx + "_b2b", [128, 8], F32)
    eold = mem.alloc(pfx + "_eold", [128, 4, 1], BF16)
    enew = mem.alloc(pfx + "_enew", [128, 4, 1], BF16)
    zrec = mem.alloc(pfx + "_zrec", [2, 512], F32)
    zr65 = mem.alloc(pfx + "_zr65", [65, 512], F32)
    marena = mem.mark()
    wqkv = mem.alloc(pfx + "_wqkvb", [128, 8, 3 * inner], BF16)
    wo = mem.alloc(pfx + "_wob", [128, inner // 128, 1024], BF16)
    kT = mem.alloc(pfx + "_kT", [128, pairs, ring * 128], BF16)
    vt = mem.alloc(pfx + "_vt", [128, ring, inner], BF16)
    qT = mem.alloc(pfx + "_qT", [128, pairs, 512], BF16)
    oT = mem.alloc(pfx + "_oT", [128, pairs, 512], BF16)
    pt = mem.alloc(pfx + "_pt", [128, 2048], BF16)
    sctmp = mem.alloc(pfx + "_sctmp", [128, 512], F32)
    attn_end = mem.mark()

    def yshift(c, t0, T):
        off = 1 + t0 - (1 if c >= 4 else 0)
        return View(y.h[:, c, off:off + T], y.name, (c * 4104 + off, c * 4104 + off + T))

    def ycur(c, t0, T):
        off = 1 + t0
        return View(y.h[:, c, off:off + T], y.name, (c * 4104 + off, c * 4104 + off + T))

    vtflat = vt.h[:, :, :].rearrange("p a b -> p (a b)")
    for tc in range(8):
        t0 = 512 * tc
        P.dma(View(vtflat[:, 0:4096].rearrange("p (c t) -> p c t", t=512), vt.name, None),
              View(xin.t.ap().rearrange("c p t -> p c t")[:, :, t0:t0 + 512],
                   xin.name, None), queue="pool")
        ln_fm(P, G, lambda c: View(vtflat[:, c * 512:(c + 1) * 512], vt.name,
                                   (c * 512 * 2, (c + 1) * 512 * 2)),
              lambda c: ycur(c, t0, 512), 512)
    zcol = View(y.h[:, :, 0:1], y.name, None)
    P.memset(zcol, 0.0)

    def load_tabs(t0):
        for n in tabs:
            P.dma(tabs[n].v, tabd[n][:, t0:t0 + 512], queue="sp")

    def qk_norm_rope(psum_in, stv, cosv, sinv, outs, eps_col=2):
        sq = G.sq[:, 0, :]
        P.act(sq, psum_in, AF.Square)
        n2 = ps.bank(7, p0=0, p1=2)
        P.mm(n2, G.c["ones2_b"].v, sq, start=True, stop=True)
        P.act(zrec.v, n2, AF.Ln, bias=G.c["eps_f"][0:2, eps_col:eps_col + 1])
        P.act(zrec.v, zrec.v, AF.Exp, scale=-0.5)
        nb_ = ps.bank(5)
        P.mm(nb_, stv, zrec.v, start=True, stop=True)
        a0 = View(G.tmpa.h[:, 0, :], G.tmpa.name, (0, 512))
        a1 = View(G.tmpa.h[:, 1, :], G.tmpa.name, (512, 1024))
        b0 = View(G.tmpb.h[:, 0, :], G.tmpb.name, (0, 512))
        b1_ = View(G.tmpb.h[:, 1, :], G.tmpb.name, (512, 1024))
        P.act(a0, psum_in, AF.Copy)
        P.tt(b0, a0, nb_, ALU.mult)
        rot = ps.bank(7)
        P.mm(rot, G.c["rot_f"].v, b0, start=True, stop=True)
        P.tt(a1, b0, cosv, ALU.mult)
        P.tt(b1_, rot, sinv, ALU.mult)
        for (ov, c0, w) in outs:
            P.tt(ov, View(G.tmpa.h[:, 1, c0:c0 + w], G.tmpa.name, (512 + c0, 512 + c0 + w)),
                 View(G.tmpb.h[:, 1, c0:c0 + w], G.tmpb.name, (512 + c0, 512 + c0 + w)),
                 ALU.add)

    for li in range(depth):
        P.dma(wqkv.v, View(wqkvd.t.ap()[li].rearrange("(c p) j -> p c j", p=128),
                           wqkvd.name, None), queue="pool")
        P.dma(wo.v, View(wod.t.ap()[li].rearrange("(c p) j -> p c j", p=128),
                         wod.name, None), queue="pool")
        P.dma(qst.v, qstd[li, :, :], queue="sp")
        P.dma(kst.v, kstd[li, :, :], queue="sp")
        P.dma(lng.v, lngd[li, :, :], queue="sp")
        P.dma(lnb.v, lnbd[li, :, :], queue="sp")
        P.dma(b1a.v, b1ad[li, :, :], queue="sp")
        P.dma(b1g.v, b1gd[li, :, :], queue="sp")
        P.dma(b2.v, b2d[li, :, :], queue="sp")

        for g in range(8):
            t0 = 512 * g
            load_tabs(t0)
            for tb in range(4):
                tg = 4 * g + tb
                slot = tg % ring
                nbanks = inner // 512
                vp = [ps.bank(4 + b2i, c0=0, c1=512) for b2i in range(nbanks)]
                for half in range(nbanks):
                    for c in range(8):
                        P.mm(vp[half], yshift(c, t0 + tb * 128, 128),
                             wqkv[:, c, 2 * inner + half * 512: 2 * inner + (half + 1) * 512],
                             start=(c == 0), stop=(c == 7))
                P.copy(vt[:, slot, :], ps.banks(4, nbanks, c1=inner))
            for p in range(pairs):
                kp = ps.bank(6)
                for c in range(8):
                    P.mm(kp, wqkv[:, c, inner + p * 128: inner + (p + 1) * 128],
                         yshift(c, t0, 512), start=(c == 0), stop=(c == 7))
                s0 = (4 * g) % ring
                outs = []
                run = min(ring - s0, 4)
                outs.append((View(kT.h[:, p, s0 * 128:(s0 + run) * 128], kT.name,
                                  (p * ring * 128 + s0 * 128, p * ring * 128 + (s0 + run) * 128)),
                            0, run * 128))
                if run < 4:
                    r2 = 4 - run
                    outs.append((View(kT.h[:, p, 0:r2 * 128], kT.name,
                                      (p * ring * 128, p * ring * 128 + r2 * 128)),
                                run * 128, r2 * 128))
                qk_norm_rope(kp, kst.v, tabs["kcos"].v, tabs["ksin"].v, outs)
            for p in range(pairs):
                qp = ps.bank(6)
                for c in range(8):
                    P.mm(qp, wqkv[:, c, p * 128:(p + 1) * 128],
                         yshift(c, t0, 512), start=(c == 0), stop=(c == 7))
                qk_norm_rope(qp, qst.v, tabs["qcos"].v, tabs["qsin"].v,
                             [(qT[:, p, :], 0, 512)])
            for p in range(pairs):
                for qt in range(4):
                    tg = 4 * g + qt
                    kcs = [t for t in range(tg - nb, tg + 1) if t >= 0]
                    for hh in range(2):
                        rows = slice(64 * hh, 64 * hh + 64)
                        for j, kt_ in enumerate(kcs):
                            slot = kt_ % ring
                            scb = ps.bank(2 * hh + j // 4, c0=(j % 4) * 128,
                                          c1=(j % 4) * 128 + 128)
                            P.mm(scb, kT[rows, p, slot * 128:(slot + 1) * 128],
                                 View(qT.h[rows, p, qt * 128:(qt + 1) * 128], qT.name,
                                      (p * 512 + qt * 128, p * 512 + (qt + 1) * 128)),
                                 start=True, stop=True)
                    nkc = len(kcs)
                    for hh in range(2):
                        P.act(pt[:, hh * 1024: hh * 1024 + nkc * 128],
                              ps.banks(2 * hh, 2, c1=nkc * 128), AF.Exp, scale=8.0)
                    for hh in range(2):
                        base = hh * 1024
                        if nkc == nb + 1:
                            dv = pt[:, base:base + 128]
                            P.tt(dv, dv, G.c["tri2_b"].v, ALU.mult, engine="dve")
                        dv = pt[:, base + (nkc - 1) * 128: base + nkc * 128]
                        P.tt(dv, dv, G.c["tri_b"].v, ALU.mult, engine="dve")
                    for hh in range(2):
                        h = 2 * p + hh
                        for j, kt_ in enumerate(kcs):
                            slot = kt_ % ring
                            P.mm(ps.bank(4, p0=64 * hh, p1=64 * hh + 64,
                                         c0=qt * 128, c1=(qt + 1) * 128),
                                 vt[:, slot, h * 64:(h + 1) * 64],
                                 pt[:, hh * 1024 + j * 128: hh * 1024 + (j + 1) * 128],
                                 start=(j == 0), stop=(j == len(kcs) - 1))
                            P.mm(ps.bank(5, p0=64 * hh, p1=64 * hh + 1,
                                         c0=qt * 128, c1=(qt + 1) * 128),
                                 G.c["ones_col_b"].v,
                                 pt[:, hh * 1024 + j * 128: hh * 1024 + (j + 1) * 128],
                                 start=(j == 0), stop=(j == len(kcs) - 1))
                P.act(zr65[0:1, :], ps.bank(5, p0=0, p1=1), AF.Ln)
                P.act(zr65[0:1, :], zr65[0:1, :], AF.Exp, scale=-1.0)
                P.act(zr65[64:65, :], ps.bank(5, p0=64, p1=65), AF.Ln)
                P.act(zr65[64:65, :], zr65[64:65, :], AF.Exp, scale=-1.0)
                zb = ps.bank(6)
                P.mm(ps.bank(6, p0=0, p1=64), G.c["ones65_f"][0:1, :], zr65[0:1, :],
                     start=True, stop=True)
                P.mm(ps.bank(6, p0=64, p1=128), G.c["ones65_f"][64:65, :], zr65[64:65, :],
                     start=True, stop=True)
                P.act(sctmp.v, ps.bank(4), AF.Copy)
                P.tt(oT[:, p, :], sctmp.v, zb, ALU.mult)
            if g > 0:
                P.copy(View(y.h[:, 4:8, t0:t0 + 1], y.name,
                            (4 * 4104 + t0, 7 * 4104 + t0 + 1)), enew.v)
            if g < 7:
                P.copy(eold.v, View(y.h[:, 4:8, t0 + 512:t0 + 513], y.name,
                                    (4 * 4104 + t0 + 512, 7 * 4104 + t0 + 513)))
            for jc in range(8):
                pj = ps.bank(4 + jc % 2)
                for c in range(inner // 128):
                    P.mm(pj, wo[:, c, jc * 128:(jc + 1) * 128], oT[:, c, :],
                         start=(c == 0), stop=(c == inner // 128 - 1))
                yv = ycur(jc, t0, 512)
                P.tt(yv, pj, yv, ALU.add)
            if g < 7:
                ev = View(y.h[:, 4:8, t0 + 512:t0 + 513], y.name,
                          (4 * 4104 + t0 + 512, 7 * 4104 + t0 + 513))
                P.copy(enew.v, ev)
                P.copy(ev, eold.v)
        P.barrier()
        mem.reset(marena)
        xnf = mem.alloc(pfx + "_xnf", [128, 8, 1024], BF16)
        hT = mem.alloc(pfx + "_hT", [128, 22, 1024], BF16)
        w1s = mem.alloc(pfx + "_w1s", [128, 1, 8, 512], BF16)
        w2r = mem.alloc(pfx + "_w2r", [128, 22, 1024], BF16)
        P.dma(w2r.v, View(w2d.t.ap()[li].rearrange("(c p) j -> p c j", p=128),
                          w2d.name, None), queue="pool")
        pieces = [(0, 512), (512, 512), (1024, 512), (1536, 512), (2048, 512), (2560, 256)]
        for tj in range(4):
            t0 = 1024 * tj
            for hf in range(2):
                tt0 = t0 + hf * 512
                ln_fm(P, G, lambda c: yshift(c, tt0, 512),
                      lambda c: View(xnf.h[:, c, hf * 512:hf * 512 + 512], xnf.name,
                                     (c * 1024 + hf * 512, c * 1024 + hf * 512 + 512)),
                      512,
                      gains=(lambda c: lng[:, c:c + 1], lambda c: lnb[:, c:c + 1]))
            if tj > 0:
                P.copy(View(y.h[:, 4:8, t0:t0 + 1], y.name,
                            (4 * 4104 + t0, 7 * 4104 + t0 + 1)), enew.v)
            if tj < 3:
                P.copy(eold.v, View(y.h[:, 4:8, t0 + 1024:t0 + 1025], y.name,
                                    (4 * 4104 + t0 + 1024, 7 * 4104 + t0 + 1025)))
            for pi, (c0, w) in enumerate(pieces):
                db = 0
                P.dma(View(w1s.h[:, db, :, 0:w], w1s.name,
                           (db * 4096, db * 4096 + 8 * 512)),
                      View(w1ad.t.ap()[li].rearrange("(c p) j -> p c j", p=128)[:, :, c0:c0 + w],
                           w1ad.name, None), queue="pool")
                for sub in range(w // 128):
                    hc = c0 // 128 + sub
                    for th in range(2):
                        ab = ps.bank(hc % 2)
                        for c in range(8):
                            P.mm(ab, w1s[:, db, c, sub * 128:(sub + 1) * 128],
                                 View(xnf.h[:, c, th * 512:(th + 1) * 512], xnf.name,
                                      (c * 1024 + th * 512, c * 1024 + (th + 1) * 512)),
                                 start=(c == 0), stop=(c == 7))
                        P.copy(View(hT.h[:, hc, th * 512:(th + 1) * 512], hT.name,
                                    (hc * 1024 + th * 512, hc * 1024 + (th + 1) * 512)), ab)
                P.dma(View(w1s.h[:, db, :, 0:w], w1s.name,
                           (db * 4096, db * 4096 + 8 * 512)),
                      View(w1gd.t.ap()[li].rearrange("(c p) j -> p c j", p=128)[:, :, c0:c0 + w],
                           w1gd.name, None), queue="pool")
                for sub in range(w // 128):
                    hc = c0 // 128 + sub
                    for th in range(2):
                        gb_ = ps.bank(2 + hc % 2)
                        for c in range(8):
                            P.mm(gb_, w1s[:, db, c, sub * 128:(sub + 1) * 128],
                                 View(xnf.h[:, c, th * 512:(th + 1) * 512], xnf.name,
                                      (c * 1024 + th * 512, c * 1024 + (th + 1) * 512)),
                                 start=(c == 0), stop=(c == 7))
                        gg = View(G.tmpa.h[:, th, :], G.tmpa.name, (th * 512, (th + 1) * 512))
                        P.act(gg, gb_, AF.Gelu, bias=b1g[:, hc:hc + 1])
                        hv = View(hT.h[:, hc, th * 512:(th + 1) * 512], hT.name,
                                  (hc * 1024 + th * 512, hc * 1024 + (th + 1) * 512))
                        P.stt(hv, hv, b1a[:, hc:hc + 1], gg, ALU.add, ALU.mult)
            for jc in range(8):
                for th in range(2):
                    dp = ps.bank(4 + th)
                    for f in range(22):
                        P.mm(dp, w2r[:, f, jc * 128:(jc + 1) * 128],
                             View(hT.h[:, f, th * 512:(th + 1) * 512], hT.name,
                                  (f * 1024 + th * 512, f * 1024 + (th + 1) * 512)),
                             start=(f == 0), stop=(f == 21))
                    yv = ycur(jc, t0 + th * 512, 512)
                    P.stt(yv, dp, b2[:, jc:jc + 1], yv, ALU.add, ALU.add)
            if tj < 3:
                ev = View(y.h[:, 4:8, t0 + 1024:t0 + 1025], y.name,
                          (4 * 4104 + t0 + 1024, 7 * 4104 + t0 + 1025))
                P.copy(enew.v, ev)
                P.copy(ev, eold.v)
        P.barrier()
        mem.reset(marena)
        if li + 1 < depth:
            pass
    for tc in range(8):
        t0 = 512 * tc
        for jc in range(8):
            xr = View(G.tmpa.h[:, 0, :], G.tmpa.name, (0, 512))
            P.dma(xr, xin[jc, :, t0:t0 + 512], queue="sp")
            rv = View(G.tmpb.h[:, 0, :], G.tmpb.name, (0, 512))
            P.tt(rv, ycur(jc, t0, 512), xr, ALU.add)
            P.dma(View(xout.t.ap()[jc, :, t0:t0 + 512], xout.name,
                       (jc * 128 * 4096 + t0, jc * 128 * 4096 + t0 + 512)),
                  rv, queue="sp", output=True)
    mem.reset(m0)


INNER_CONSTS = ["ones_col_b", "ones_col_f", "ones_row_f", "ones2_b", "rot_f",
                "tri_b", "tri2_b", "esel2_f", "ones65_f", "eps_f", "ident_f"]


def make_l2f_program():
    nc = bass.Bass("TRN2", target_bir_lowering=False)
    P = Prog(nc)
    mem = Mem(nc)
    G = Globals(P, mem, INNER_CONSTS)
    xin = DBuf(nc, "xin", [8, 128, 4096], F32, kind="ExternalInput")
    xout = DBuf(nc, "xout", [8, 128, 4096], F32, kind="ExternalOutput")
    build_inner_tf(P, G, mem, xin, xout, "l2f", 6, 16, 256)
    P.fence_outputs()
    P.emit()
    return nc, G


def make_l3f_program():
    nc = bass.Bass("TRN2", target_bir_lowering=False)
    P = Prog(nc)
    mem = Mem(nc)
    G = Globals(P, mem, INNER_CONSTS)
    xin = DBuf(nc, "xin", [8, 128, 4096], F32, kind="ExternalInput")
    xout = DBuf(nc, "xout", [8, 128, 4096], F32, kind="ExternalOutput")
    build_inner_tf(P, G, mem, xin, xout, "l3f", 2, 8, 512)
    P.fence_outputs()
    P.emit()
    return nc, G



def host_prep_l3_attn(inp):
    return {
        "l3_wq": np.ascontiguousarray(inp["L3_attn_wq"].transpose(1, 0, 2).reshape(1024, 1024)),
        "l3_wk": np.ascontiguousarray(inp["L3_attn_wk"].transpose(1, 0, 2).reshape(1024, 1024)),
        "l3_wv": np.ascontiguousarray(inp["L3_attn_wv"].transpose(1, 0, 2).reshape(1024, 1024)),
        "l3_wo": np.ascontiguousarray(inp["L3_attn_wo"]),
        "l3_lnw": np.ascontiguousarray(inp["L3_attn_ln_w"].reshape(8, 128).T),
        "l3_lnb": np.ascontiguousarray(inp["L3_attn_ln_b"].reshape(8, 128).T),
    }


def build_l3_attn(P, G, mem, xin, xout):
    nc = P.nc
    ps = G.psum
    wq = DBuf(nc, "l3_wq", [1024, 1024], F32, kind="ExternalInput")
    wk = DBuf(nc, "l3_wk", [1024, 1024], F32, kind="ExternalInput")
    wv = DBuf(nc, "l3_wv", [1024, 1024], F32, kind="ExternalInput")
    wo = DBuf(nc, "l3_wo", [1024, 1024], F32, kind="ExternalInput")
    lnwd = DBuf(nc, "l3_lnw", [128, 8], F32, kind="ExternalInput")
    lnbd = DBuf(nc, "l3_lnb", [128, 8], F32, kind="ExternalInput")
    xnsp = DBuf(nc, "l3_xnspill", [8, 128, 4096], BF16, kind="Internal")
    y2sp = DBuf(nc, "l3_y2spill", [8, 128, 4096], BF16, kind="Internal")

    m0 = mem.mark()
    KT = mem.alloc("l3_KT", [128, 8, 4096], BF16)
    V = mem.alloc("l3_V", [128, 32, 1040], BF16)
    w1 = mem.alloc("l3_w1", [128, 8, 1024], BF16)
    w2 = mem.alloc("l3_w2", [128, 8, 1024], BF16)
    xc = mem.alloc("l3_xc", [128, 8, 512], BF16)
    lnw = mem.alloc("l3_lnwb", [128, 8], F32)
    lnb = mem.alloc("l3_lnbb", [128, 8], F32)
    qt = mem.alloc("l3_qt", [128, 512], BF16)
    pt = mem.alloc("l3_pt", [128, 2048], BF16)
    ot = mem.alloc("l3_ot", [128, 8, 512], BF16)
    mtmp = mem.alloc("l3_mtmp", [128, 16], F32)
    crow = mem.alloc("l3_crow", [1, 512], F32)
    xn2 = mem.alloc("l3_xn2", [128, 1, 512], BF16)
    xr32 = mem.alloc("l3_xr32", [128, 2, 512], F32)

    def ldw(dst, src):
        P.dma(dst.v, src.rearranged("(c p) j -> p c j", p=128), queue="pool")

    ldw(w1, wk)
    ldw(w2, wv)
    P.dma(lnw.v, lnwd.v, queue="sp")
    P.dma(lnb.v, lnbd.v, queue="sp")

    for tc in range(8):
        t0 = 512 * tc
        src = View(xin.t.ap().rearrange("c p t -> p c t")[:, :, t0:t0 + 512],
                   xin.name, None)
        P.dma(xc.v, src, queue="pool")
        ln_fm(P, G, lambda c: xc[:, c, :], lambda c: xc[:, c, :], 512)
        dst = View(xnsp.t.ap().rearrange("c p t -> p c t")[:, :, t0:t0 + 512],
                   xnsp.name, (t0, t0 + 512))
        P.dma(dst, xc.v, queue="sp")
        ln_fm(P, G, lambda c: xc[:, c, :], lambda c: xc[:, c, :], 512,
              gains=(lambda c: lnw[:, c:c + 1], lambda c: lnb[:, c:c + 1]))
        dst = View(y2sp.t.ap().rearrange("c p t -> p c t")[:, :, t0:t0 + 512],
                   y2sp.name, (t0, t0 + 512))
        P.dma(dst, xc.v, queue="sp")
        for p in range(8):
            kb = ps.bank(6)
            for c in range(8):
                P.mm(kb, w1[:, c, p * 128:(p + 1) * 128], xc[:, c, :],
                     start=(c == 0), stop=(c == 7))
            P.act(KT[:, p, t0:t0 + 512], kb, AF.Copy)
        for tb in range(4):
            ttg = tc * 4 + tb
            pv0 = ps.bank(0, c0=0, c1=512)
            pv1 = ps.bank(1, c0=0, c1=512)
            for c in range(8):
                stat = xc[:, c, tb * 128:(tb + 1) * 128]
                P.mm(pv0, stat, w2[:, c, 0:512], start=(c == 0), stop=(c == 7))
                P.mm(pv1, stat, w2[:, c, 512:1024], start=(c == 0), stop=(c == 7))
            srcv = View(ps.f32[0:128, 0:1024].rearrange("p (h e) -> p h e", e=64),
                        ps.name, (0, 4096))
            dstv = View(V.h[:, ttg, 0:1040].rearrange("p (h e) -> p h e", e=65)[:, :, 0:64],
                        V.name, (ttg * 1040 * 2, (ttg + 1) * 1040 * 2))
            P.copy(dstv, srcv)
    vones = View(V.h[:, :, :].rearrange("p t (h e) -> p t h e", e=65)[:, :, :, 64:65],
                 V.name, None)
    P.memset(vones, 1.0)

    ldw(w1, wq)
    ldw(w2, wo)
    for qb in range(8):
        q0 = 512 * qb
        src = View(y2sp.t.ap().rearrange("c p t -> p c t")[:, :, q0:q0 + 512],
                   y2sp.name, (q0, q0 + 512))
        P.dma(xc.v, src, queue="sp")
        for p in range(8):
            qb6 = ps.bank(6)
            for c in range(8):
                P.mm(qb6, w1[:, c, p * 128:(p + 1) * 128], xc[:, c, :],
                     start=(c == 0), stop=(c == 7))
            P.act(qt.v, qb6, AF.Copy)
            for hh in range(2):
                rows = slice(64 * hh, 64 * hh + 64)
                for qs in range(4):
                    nkc = qb + 1
                    for kc5 in range(nkc):
                        lastw = 512 if kc5 < qb else (qs + 1) * 128
                        tm = ps.bank(7, c0=0, c1=lastw)
                        P.mm(tm, qt[rows, qs * 128:(qs + 1) * 128],
                             KT[rows, p, kc5 * 512:kc5 * 512 + lastw],
                             start=True, stop=True)
                        if kc5 == qb:
                            dv = ps.bank(7, c0=qs * 128, c1=(qs + 1) * 128)
                            P.tt(dv, dv, G.c["trineg_f"].v, ALU.add)
                        P.reduce(mtmp[:, kc5:kc5 + 1], tm, ALU.max)
                    P.reduce(mtmp[:, 9:10], mtmp[:, 0:qb + 1], ALU.max, negate=True)
                    ctp = ps.bank(5, p0=0, p1=1, c0=qs * 128, c1=(qs + 1) * 128)
                    P.transpose(ctp, mtmp[:, 9:10], G.c["ident_f"].v)
                P.copy(crow.v, ps.bank(5, p0=0, p1=1, c0=0, c1=512))
                nkc = 4 * (qb + 1)
                for gidx in range(qb + 1):
                    for j in range(4):
                        kc = 4 * gidx + j
                        scb = ps.bank(j)
                        P.mm(scb, KT[rows, p, kc * 128:(kc + 1) * 128], qt[rows, :],
                             start=True, stop=False)
                        P.mm(scb, G.c["onesr_f"].v, crow.v, start=False, stop=True)
                    P.act(pt.v, ps.banks(0, 4), AF.Exp, scale=0.125)
                    if gidx == qb:
                        for j in range(4):
                            if j > 0:
                                P.memset(pt[:, j * 512: j * 512 + j * 128], 0.0)
                            dv = pt[:, j * 512 + j * 128: j * 512 + (j + 1) * 128]
                            P.tt(dv, dv, G.c["tri_b"].v, ALU.mult, engine="dve")
                    for j in range(4):
                        kc = 4 * gidx + j
                        P.mm(ps.bank(4, p0=0, p1=65),
                             V[:, kc, p * 130 + hh * 65: p * 130 + hh * 65 + 65],
                             pt[:, j * 512:(j + 1) * 512],
                             start=(kc == 0), stop=(kc == nkc - 1))
                z0 = G.rows[0:1, 0:512]
                z1 = G.rows[0:1, 512:1024]
                P.ts(z0, ps.bank(4, p0=64, p1=65), 1e-30, ALU.add)
                P.recip(z1, z0)
                P.mm(ps.bank(6, p0=0, p1=64), G.c["ones64_f"].v, z1,
                     start=True, stop=True)
                otv = View(G.tmpa.h[0:64, 0, :], G.tmpa.name, (0, 512))
                P.act(otv, ps.bank(4, p0=0, p1=64), AF.Copy)
                P.tt(ot[rows, p, :], otv, ps.bank(6, p0=0, p1=64), ALU.mult)
        for jc in range(8):
            P.dma(xr32[:, jc % 2, :], xin[jc, :, q0:q0 + 512], queue="sp")
            P.dma(xn2[:, 0, :], View(xnsp.t.ap()[jc, :, q0:q0 + 512], xnsp.name,
                                     (q0, q0 + 512)), queue="sp")
            pj = ps.bank(jc % 2)
            for c in range(8):
                P.mm(pj, w2[:, c, jc * 128:(jc + 1) * 128], ot[:, c, :],
                     start=(c == 0), stop=(c == 7))
            rv = xr32[:, jc % 2, :]
            P.tt(rv, pj, rv, ALU.add)
            P.tt(rv, xn2[:, 0, :], rv, ALU.add)
            P.dma(View(xout.t.ap()[jc, :, q0:q0 + 512], xout.name,
                       (jc * 128 * 4096 + q0, jc * 128 * 4096 + q0 + 512)),
                  rv, queue="sp", output=True)
    mem.reset(m0)


L3A_CONSTS = ["ones_col_b", "ones_col_f", "ones_row_f", "ones64_f", "tri_b",
              "trineg_f", "onesr_f", "ident_f", "eps_f"]


def make_l3a_program():
    nc = bass.Bass("TRN2", target_bir_lowering=False)
    P = Prog(nc)
    mem = Mem(nc)
    G = Globals(P, mem, L3A_CONSTS)
    xin = DBuf(nc, "xin", [8, 128, 4096], F32, kind="ExternalInput")
    xout = DBuf(nc, "xout", [8, 128, 4096], F32, kind="ExternalOutput")
    build_l3_attn(P, G, mem, xin, xout)
    P.fence_outputs()
    P.emit()
    return nc, G



_PROG_CACHE = {}


def _get_programs():
    if "progs" in _PROG_CACHE:
        return _PROG_CACHE["progs"]
    progs = []
    for name, mk in (("l0a", make_l0a_program), ("l0f", make_l0f_program),
                     ("l1a", make_l1a_program), ("l1f", make_l1f_program),
                     ("l2a", make_l2a_program), ("l2f", make_l2f_program),
                     ("l3a", make_l3a_program), ("l3f", make_l3f_program)):
        progs.append((name,) + mk())
    _PROG_CACHE["progs"] = progs
    return progs


def kernel(**inputs):
    from concourse.compiler_utils import temporarily_append_compiler_flags
    progs = _get_programs()
    preps = {
        "l0a": host_prep_l0_attn(inputs),
        "l0f": host_prep_l0_ffn(inputs),
        "l1a": host_prep_l1_attn(inputs),
        "l1f": host_prep_l1_ffn(inputs),
        "l2a": host_prep_l2_attn(inputs),
        "l2f": host_prep_inner(inputs, "l2f", "", 6, 16, 256),
        "l3a": host_prep_l3_attn(inputs),
        "l3f": host_prep_inner(inputs, "l3f", "", 2, 8, 512),
    }
    x = np.asarray(inputs["L0_attn_x"])
    cur = [np.ascontiguousarray(x[b].T).reshape(8, 128, 4096).astype(np.float32)
           for b in range(8)]
    with temporarily_append_compiler_flags(["--optlevel", "1"]):
        for (name, nc, G) in progs:
            wp = preps[name]
            in_maps = [dict(xin=cur[b], **wp, **G.host_consts) for b in range(8)]
            res = run_bass_kernel_spmd(nc, in_maps, core_ids=list(range(8)))
            cur = [np.asarray(res.results[b]["xout"]) for b in range(8)]
    return np.stack([cur[b].reshape(1024, 4096).T for b in range(8)])
